# version 01
# baseline (speedup 1.0000x reference)
import math
import contextlib
import numpy as np
import ml_dtypes
import concourse.bass as bass
import concourse.mybir as mybir
from concourse.bass_utils import run_bass_kernel_spmd
try:
    from concourse.bass_core_rust import IndirectOffsetOnAxis
except Exception:
    from bass_rust import IndirectOffsetOnAxis

F32 = mybir.dt.float32
BF16 = mybir.dt.bfloat16
I32 = mybir.dt.int32
AF = mybir.ActivationFunctionType
ALU = mybir.AluOpType
AX = mybir.AxisListType
BF = ml_dtypes.bfloat16

NC = 8
B = 2
L = 8192
DM = 1024
TK = L * B // NC
ED = 2048
NST = 32
DTR = 64
NB = 4
ESL = ED // NC
TT = B * L
CAP0 = 4096
CAP1 = 1024
NE = 8


class Seq:
    def __init__(self, nc, block, sem):
        self.block, self.sem = block, sem
        self.goal = 0
        self.pend = 0
        self.waited = {}

    def issue(self, engine_name, builder, incs=1):
        need = self.goal
        seq = self
        @getattr(self.block, engine_name)
        def _(eng):
            if need > 0 and seq.waited.get(engine_name, 0) < need:
                eng.wait_ge(seq.sem, need)
                seq.waited[engine_name] = need
            last = builder(eng)
            if last is not None and incs:
                last.then_inc(seq.sem, incs)
        if incs:
            self.pend += incs

    def dma(self, engine_name, out, in_, **kw):
        self.issue(engine_name, lambda e: e.dma_start(out=out, in_=in_, **kw), incs=16)

    def barrier(self):
        self.goal += self.pend
        self.pend = 0

    def finish(self):
        self.barrier()
        need = self.goal
        @self.block.sync
        def _(sync):
            sync.wait_ge(self.sem, need)


@contextlib.contextmanager
def program(name):
    nc = bass.Bass("TRN2", target_bir_lowering=False)
    es = contextlib.ExitStack()
    with es:
        holder = {}
        def sb(nm, shape, dtype=F32):
            return es.enter_context(nc.sbuf_tensor(nm, shape, dtype))
        def ps(nm, shape, dtype=F32):
            return es.enter_context(nc.psum_tensor(nm, shape, dtype))
        def begin():
            sem = es.enter_context(nc.semaphore("seq"))
            block = es.enter_context(nc.Block())
            s = Seq(nc, block, sem)
            holder["s"] = s
            return s
        yield nc, sb, ps, begin
        holder["s"].finish()


def din(nc, name, shape, dtype=F32):
    return nc.dram_tensor(name, shape, dtype, kind="ExternalInput").ap()

def dout(nc, name, shape, dtype=F32):
    return nc.dram_tensor(name, shape, dtype, kind="ExternalOutput").ap()



def replicate_row(s, row_ap, dst, p_r, t):
    nf = (t + 511) // 512
    for g in range(0, nf, 4):
        for f in range(g, min(g + 4, nf)):
            w = min(512, t - f*512)
            sl = (f % 4) * 512
            s.issue("tensor", lambda te, f=f, w=w, sl=sl: te.matmul(
                p_r[:, sl:sl+w], s.ones_row[0:1, :], row_ap[0:1, f*512:f*512+w],
                start=True, stop=True))
        s.barrier()
        for f in range(g, min(g + 4, nf)):
            w = min(512, t - f*512)
            sl = (f % 4) * 512
            s.issue("vector", lambda v, f=f, w=w, sl=sl: v.tensor_copy(
                dst[:, f*512:f*512+w], p_r[:, sl:sl+w]))
        s.barrier()


def ln_stats(s, sb, psA, psB, xt, t, pre, eps=1e-5, nd=8, dcount=None):
    dcount = dcount or nd * 128
    sq = sb(pre + "sq", [128, t])
    p_s = psA[0:1, :]
    srow = sb(pre + "srow", [1, t])
    qrow = sb(pre + "qrow", [1, t])
    stat = sb(pre + "stat", [1, t])
    mrow = sb(pre + "mrow", [1, t])
    r_b = sb(pre + "rb", [128, t])
    mr_b = sb(pre + "mrb", [128, t])
    p_r = psB
    nf = t // 512

    for f in range(nf):
        def mm_s(tensor, f=f):
            last = None
            for k in range(nd):
                last = tensor.matmul(p_s[:, f*512:(f+1)*512], s.ones_col[:, 0:1],
                                     xt[k][:, f*512:(f+1)*512], start=(k == 0), stop=(k == nd-1))
            return last
        s.issue("tensor", mm_s)
    s.barrier()
    s.issue("vector", lambda v: v.tensor_copy(srow[:], p_s[0:1, :]))
    s.barrier()
    for k in range(nd):
        s.issue("scalar", lambda sc, k=k: sc.activation(out=sq[:], in_=xt[k][:], func=AF.Square))
        s.barrier()
        for f in range(nf):
            s.issue("tensor", lambda te, k=k, f=f: te.matmul(
                p_s[:, f*512:(f+1)*512], s.ones_col[:, 0:1], sq[:, f*512:(f+1)*512],
                start=(k == 0), stop=(k == nd-1)))
        s.barrier()
    s.issue("vector", lambda v: v.tensor_copy(qrow[:], p_s[0:1, :]))
    s.barrier()
    inv = 1.0 / dcount
    s.issue("vector", lambda v: v.tensor_scalar_mul(mrow[:], srow[:], inv))
    s.barrier()
    s.issue("vector", lambda v: v.tensor_scalar_mul(qrow[:], qrow[:], inv))
    s.barrier()
    s.issue("vector", lambda v: v.tensor_mul(stat[:], mrow[:], mrow[:]))
    s.barrier()
    s.issue("vector", lambda v: v.tensor_sub(stat[:], qrow[:], stat[:]))
    s.barrier()
    s.issue("vector", lambda v: v.tensor_scalar_add(stat[:], stat[:], eps))
    s.barrier()
    s.issue("scalar", lambda sc: sc.activation(out=stat[:], in_=stat[:], func=AF.Sqrt))
    s.barrier()
    s.issue("vector", lambda v: v.reciprocal(stat[:], stat[:]))
    s.barrier()
    s.issue("vector", lambda v: v.tensor_mul(mrow[:], mrow[:], stat[:]))
    s.barrier()
    replicate_row(s, stat, r_b, p_r, t)
    replicate_row(s, mrow, mr_b, p_r, t)
    return r_b, mr_b


def setup_consts(nc, s, sb):
    onesc = din(nc, "c_ones_col", [128, 1])
    onesr = din(nc, "c_ones_row", [1, 128])
    s.ones_col = sb("ones_col", [128, 1])
    s.ones_row = sb("ones_row", [1, 128])
    s.dma("sync", s.ones_col[:], onesc)
    s.dma("sync", s.ones_row[:], onesr)
    s.barrier()

CONSTS = {
    "c_ones_col": np.ones((128, 1), np.float32),
    "c_ones_row": np.ones((1, 128), np.float32),
}



def build_entry():
    with program("entry") as (nc, sb, ps, begin):
        xT = din(nc, "xT", [DM, TK])
        WiT = din(nc, "WiT", [DM, DM])
        biR = din(nc, "biR", [128, 8])
        o_xn = dout(nc, "xn_fm", [DM, TK], BF16)
        o_h = dout(nc, "h_fm", [DM, TK])

        s = begin()
        setup_consts(nc, s, sb)
        psA = ps("psA", [128, 2048])
        psB = ps("psB", [128, 2048])
        xt = [sb(f"x{k}", [128, TK]) for k in range(8)]
        for k in range(8):
            s.dma("sync", xt[k][:], xT[k*128:(k+1)*128, :])
        wt = [sb(f"w{k}", [128, DM], BF16) for k in range(8)]
        for k in range(8):
            s.dma("gpsimd", wt[k][:], WiT[k*128:(k+1)*128, :])
        sbi = sb("sbi", [128, 8])
        s.dma("sync", sbi[:], biR)
        s.barrier()

        r_b, mr_b = ln_stats(s, sb, psA, psB, xt, TK, "ln")
        xnb = [sb(f"xnb{k}", [128, TK], BF16) for k in range(8)]
        for k in range(8):
            s.issue("vector", lambda v, k=k: v.tensor_mul(xt[k][:], xt[k][:], r_b[:]))
        s.barrier()
        for k in range(8):
            s.issue("vector", lambda v, k=k: v.tensor_sub(xt[k][:], xt[k][:], mr_b[:]))
        s.barrier()
        for k in range(8):
            s.issue("vector", lambda v, k=k: v.tensor_copy(xnb[k][:], xt[k][:]))
        s.barrier()
        for k in range(8):
            s.dma("sync", o_xn[k*128:(k+1)*128, :], xnb[k][:])
        p_h = psA
        hrow = sb("hrow", [128, TK])
        for m in range(8):
            for f in range(4):
                def mm(tensor, m=m, f=f):
                    last = None
                    for k in range(8):
                        last = tensor.matmul(p_h[:, 0:512], wt[k][:, m*128:(m+1)*128],
                                             xnb[k][:, f*512:(f+1)*512],
                                             start=(k == 0), stop=(k == 7))
                    return last
                s.issue("tensor", mm)
                s.barrier()
                s.issue("scalar", lambda sc, m=m, f=f: sc.activation(
                    out=hrow[:, f*512:(f+1)*512], in_=p_h[:, 0:512],
                    func=AF.Identity, bias=sbi[:, m:m+1]))
                s.barrier()
            s.dma("sync", o_h[m*128:(m+1)*128, :], hrow[:])
            s.barrier()
    return nc




def build_alpha():
    TH = TK + 3
    with program("alpha") as (nc, sb, ps, begin):
        hD = din(nc, "hD", [DM, TK])
        hHalo = din(nc, "hHalo", [DM, 3])
        lnG = din(nc, "lnG", [128, 8])
        lnB = din(nc, "lnB", [128, 8])
        inwT = din(nc, "inwT", [DM, 2 * ED])
        convW = din(nc, "convW", [128, 64])
        convB = din(nc, "convB", [128, 16])
        xprojT = din(nc, "xprojT", [ED, 128])
        dtprojT = din(nc, "dtprojT", [DTR, ED])
        dtprojB = din(nc, "dtprojB", [128, 16])
        hmask = din(nc, "hmask", [128, 3])
        o_delta = dout(nc, "delta", [ED, TK])
        o_us = dout(nc, "u_s", [ED, TK], BF16)
        o_zs = dout(nc, "z_s", [ED, TK], BF16)
        o_bc = dout(nc, "BC", [64, TK])

        s = begin()
        setup_consts(nc, s, sb)
        psA = ps("psA", [128, 2048])
        psB = ps("psB", [128, 2048])

        ht = [sb(f"ht{k}", [128, TK]) for k in range(8)]
        hh = [sb(f"hh{k}", [128, 3]) for k in range(8)]
        for k in range(8):
            s.dma("sync", ht[k][:], hD[k*128:(k+1)*128, :])
            s.dma("sync", hh[k][:], hHalo[k*128:(k+1)*128, :])
        sg = sb("sg", [128, 8]); sbeta = sb("sbeta", [128, 8])
        scw = sb("scw", [128, 64]); scb = sb("scb", [128, 16])
        sdtb = sb("sdtb", [128, 16]); smask = sb("smask", [128, 3])
        s.dma("sync", sg[:], lnG); s.dma("sync", sbeta[:], lnB)
        s.dma("sync", scw[:], convW); s.dma("sync", scb[:], convB)
        s.dma("sync", sdtb[:], dtprojB); s.dma("sync", smask[:], hmask)
        sxp = [sb(f"sxp{k}", [128, 128], BF16) for k in range(16)]
        for k in range(16):
            s.dma("gpsimd", sxp[k][:], xprojT[k*128:(k+1)*128, :])
        sdtp = sb("sdtp", [64, ED], BF16)
        s.dma("gpsimd", sdtp[:], dtprojT)
        s.barrier()

        r_b, mr_b = ln_stats(s, sb, psA, psB, ht, TK, "ln")
        normed = [sb(f"nm{k}", [128, TK], BF16) for k in range(8)]
        for k in range(8):
            s.issue("vector", lambda v, k=k: v.tensor_mul(ht[k][:], ht[k][:], r_b[:]))
        s.barrier()
        for k in range(8):
            s.issue("vector", lambda v, k=k: v.tensor_sub(ht[k][:], ht[k][:], mr_b[:]))
        s.barrier()
        for k in range(8):
            s.issue("vector", lambda v, k=k: v.tensor_scalar(
                out=normed[k][:], in0=ht[k][:], scalar1=sg[:, k:k+1], scalar2=sbeta[:, k:k+1],
                op0=ALU.mult, op1=ALU.add))
        s.barrier()
        hrow3 = sb("hrow3", [1, 3]); hq3 = sb("hq3", [1, 3]); hst3 = sb("hst3", [1, 3])
        hm3 = sb("hm3", [1, 3]); sq3 = sb("sq3", [128, 3])
        rb3 = sb("rb3", [128, 3]); mrb3 = sb("mrb3", [128, 3])
        nh = [sb(f"nh{k}", [128, 3], BF16) for k in range(8)]
        def mm3(tensor):
            last = None
            for k in range(8):
                last = tensor.matmul(psB[0:1, 0:3], s.ones_col[:, 0:1], hh[k][:],
                                     start=(k == 0), stop=(k == 7))
            return last
        s.issue("tensor", mm3)
        s.barrier()
        s.issue("vector", lambda v: v.tensor_copy(hrow3[:], psB[0:1, 0:3]))
        s.barrier()
        for k in range(8):
            s.issue("scalar", lambda sc, k=k: sc.activation(out=sq3[:], in_=hh[k][:], func=AF.Square))
            s.barrier()
            s.issue("tensor", lambda te, k=k: te.matmul(psB[0:1, 0:3], s.ones_col[:, 0:1], sq3[:],
                                                        start=(k == 0), stop=(k == 7)))
            s.barrier()
        s.issue("vector", lambda v: v.tensor_copy(hq3[:], psB[0:1, 0:3]))
        s.barrier()
        s.issue("vector", lambda v: v.tensor_scalar_mul(hm3[:], hrow3[:], 1.0 / DM))
        s.barrier()
        s.issue("vector", lambda v: v.tensor_scalar_mul(hq3[:], hq3[:], 1.0 / DM))
        s.barrier()
        s.issue("vector", lambda v: v.tensor_mul(hst3[:], hm3[:], hm3[:]))
        s.barrier()
        s.issue("vector", lambda v: v.tensor_sub(hst3[:], hq3[:], hst3[:]))
        s.barrier()
        s.issue("vector", lambda v: v.tensor_scalar_add(hst3[:], hst3[:], 1e-5))
        s.barrier()
        s.issue("scalar", lambda sc: sc.activation(out=hst3[:], in_=hst3[:], func=AF.Sqrt))
        s.barrier()
        s.issue("vector", lambda v: v.reciprocal(hst3[:], hst3[:]))
        s.barrier()
        s.issue("vector", lambda v: v.tensor_mul(hm3[:], hm3[:], hst3[:]))
        s.barrier()
        s.issue("tensor", lambda te: te.matmul(psB[:, 0:3], s.ones_row[0:1, :], hst3[0:1, :],
                                               start=True, stop=True))
        s.barrier()
        s.issue("vector", lambda v: v.tensor_copy(rb3[:], psB[:, 0:3]))
        s.barrier()
        s.issue("tensor", lambda te: te.matmul(psB[:, 0:3], s.ones_row[0:1, :], hm3[0:1, :],
                                               start=True, stop=True))
        s.barrier()
        s.issue("vector", lambda v: v.tensor_copy(mrb3[:], psB[:, 0:3]))
        s.barrier()
        for k in range(8):
            s.issue("vector", lambda v, k=k: v.tensor_mul(hh[k][:], hh[k][:], rb3[:]))
        s.barrier()
        for k in range(8):
            s.issue("vector", lambda v, k=k: v.tensor_sub(hh[k][:], hh[k][:], mrb3[:]))
        s.barrier()
        for k in range(8):
            s.issue("vector", lambda v, k=k: v.tensor_scalar(
                out=nh[k][:], in0=hh[k][:], scalar1=sg[:, k:k+1], scalar2=sbeta[:, k:k+1],
                op0=ALU.mult, op1=ALU.add))
        s.barrier()

        wbuf = [[sb(f"wb{pp}_{k}", [128, 128], BF16) for k in range(8)] for pp in range(2)]
        u0 = []
        for e in range(16):
            htv = ht[e // 2][:].bitcast(BF16)
            off = (e % 2) * TK
            u0.append(htv[:, off:off + TK])
        uh = [sb(f"uh{e}", [128, 3], BF16) for e in range(16)]
        zscr = sb("zscr", [128, TK], BF16)
        for k in range(8):
            s.dma("gpsimd", wbuf[0][k][:], inwT[k*128:(k+1)*128, 0:128])
        s.barrier()
        for m in range(32):
            pp = m % 2
            if m + 1 < 32:
                for k in range(8):
                    s.dma("gpsimd", wbuf[1 - pp][k][:], inwT[k*128:(k+1)*128, (m+1)*128:(m+2)*128])
            for f in range(4):
                def mm(tensor, pp=pp, f=f):
                    last = None
                    for k in range(8):
                        last = tensor.matmul(psA[:, 0:512], wbuf[pp][k][:],
                                             normed[k][:, f*512:(f+1)*512],
                                             start=(k == 0), stop=(k == 7))
                    return last
                s.issue("tensor", mm)
                s.barrier()
                if m < 16:
                    s.issue("vector", lambda v, m=m, f=f: v.tensor_copy(
                        u0[m][:, f*512:(f+1)*512], psA[:, 0:512]))
                else:
                    s.issue("scalar", lambda sc, f=f: sc.activation(
                        out=zscr[:, f*512:(f+1)*512], in_=psA[:, 0:512], func=AF.Silu))
                s.barrier()
            if m < 16:
                def mmh(tensor, pp=pp):
                    last = None
                    for k in range(8):
                        last = tensor.matmul(psB[:, 8:11], wbuf[pp][k][:], nh[k][:],
                                             start=(k == 0), stop=(k == 7))
                    return last
                s.issue("tensor", mmh)
                s.barrier()
                s.issue("vector", lambda v, m=m: v.tensor_copy(uh[m][:], psB[:, 8:11]))
                s.barrier()
            else:
                s.dma("sync", o_zs[(m-16)*128:(m-15)*128, :], zscr[:])
                s.barrier()
        for e in range(16):
            s.issue("vector", lambda v, e=e: v.tensor_mul(uh[e][:], uh[e][:], smask[:]))
        s.barrier()
        acc = sb("acc", [128, TK])
        for e in range(16):
            s.issue("vector", lambda v, e=e: v.tensor_scalar(
                out=acc[:], in0=u0[e][:], scalar1=scw[:, e*4+3:e*4+4], scalar2=None,
                op0=ALU.mult))
            s.barrier()
            for j in (0, 1, 2):
                w0 = 3 - j
                s.issue("vector", lambda v, e=e, j=j, w0=w0: v.scalar_tensor_tensor(
                    out=acc[:, w0:TK], in0=u0[e][:, 0:TK-w0], scalar=scw[:, e*4+j:e*4+j+1],
                    in1=acc[:, w0:TK], op0=ALU.mult, op1=ALU.add))
                s.barrier()
            for t in range(3):
                for j in range(3 - t):
                    s.issue("vector", lambda v, e=e, t=t, j=j: v.scalar_tensor_tensor(
                        out=acc[:, t:t+1], in0=uh[e][:, t+j:t+j+1], scalar=scw[:, e*4+j:e*4+j+1],
                        in1=acc[:, t:t+1], op0=ALU.mult, op1=ALU.add))
                    s.barrier()
            s.issue("scalar", lambda sc, e=e: sc.activation(
                out=u0[e][:], in_=acc[:], func=AF.Silu, bias=scb[:, e:e+1]))
            s.barrier()
            s.dma("sync", o_us[e*128:(e+1)*128, :], u0[e][:])
        s.barrier()
        xdbl = sb("xdbl", [128, TK])
        for f in range(4):
            def mmx(tensor, f=f):
                last = None
                for k in range(16):
                    last = tensor.matmul(psA[:, 0:512], sxp[k][:], u0[k][:, f*512:(f+1)*512],
                                         start=(k == 0), stop=(k == 15))
                return last
            s.issue("tensor", mmx)
            s.barrier()
            s.issue("vector", lambda v, f=f: v.tensor_copy(xdbl[:, f*512:(f+1)*512], psA[:, 0:512]))
            s.barrier()
        s.dma("sync", o_bc[:, :], xdbl[64:128, :])
        dtb = sb("dtb", [64, TK], BF16)
        s.issue("vector", lambda v: v.tensor_copy(dtb[:], xdbl[0:64, :]))
        s.barrier()
        drow = sb("drow", [128, TK])
        for m in range(16):
            for f in range(4):
                s.issue("tensor", lambda te, m=m, f=f: te.matmul(
                    psA[:, 0:512], sdtp[:, m*128:(m+1)*128], dtb[:, f*512:(f+1)*512],
                    start=True, stop=True))
                s.barrier()
                s.issue("scalar", lambda sc, m=m, f=f: sc.activation(
                    out=drow[:, f*512:(f+1)*512], in_=psA[:, 0:512], func=AF.Exp,
                    bias=sdtb[:, m:m+1]))
                s.barrier()
            s.issue("scalar", lambda sc: sc.activation(out=drow[:], in_=drow[:], func=AF.Ln, bias=1.0))
            s.barrier()
            s.dma("sync", o_delta[m*128:(m+1)*128, :], drow[:])
            s.barrier()
    return nc


def _prep_alpha(inputs, blk, h_full):
    lnG = np.ascontiguousarray(inputs["L0_attn_ln_g"][blk].reshape(8, 128).T)
    lnB = np.ascontiguousarray(inputs["L0_attn_ln_b"][blk].reshape(8, 128).T)
    inwT = np.ascontiguousarray(inputs["L0_attn_in_w"][blk].T)
    convW = np.ascontiguousarray(
        inputs["L0_attn_conv_w"][blk][:, 0, :].reshape(16, 128, 4).transpose(1, 0, 2).reshape(128, 64))
    convB = np.ascontiguousarray(inputs["L0_attn_conv_b"][blk].reshape(16, 128).T)
    xprojT = np.ascontiguousarray(inputs["L0_attn_xproj_w"][blk].T)
    dtprojT = np.ascontiguousarray(inputs["L0_attn_dtproj_w"][blk].T)
    dtprojB = np.ascontiguousarray(inputs["L0_attn_dtproj_b"][blk].reshape(16, 128).T)
    maps = []
    for c in range(NC):
        b, q = c // 4, c % 4
        hT = np.ascontiguousarray(h_full[b, q*TK:(q+1)*TK, :].T)
        if q == 0:
            halo = np.zeros((DM, 3), np.float32)
            hm = np.zeros((128, 3), np.float32)
        else:
            halo = np.ascontiguousarray(h_full[b, q*TK-3:q*TK, :].T)
            hm = np.ones((128, 3), np.float32)
        maps.append({"hD": hT, "hHalo": halo, "lnG": lnG, "lnB": lnB, "inwT": inwT,
                     "convW": convW, "convB": convB, "xprojT": xprojT, "dtprojT": dtprojT,
                     "dtprojB": dtprojB, "hmask": hm, **CONSTS})
    return maps



def build_beta():
    TC = 512
    NCH = L // TC
    with program("beta") as (nc, sb, ps, begin):
        dS = din(nc, "dS", [ESL, TT])
        uS = din(nc, "uS", [ESL, TT], BF16)
        Bh = din(nc, "Bh", [NST, TT])
        Ch = din(nc, "Ch", [NST, TT])
        Asl = din(nc, "Asl", [128, 64])
        sel = din(nc, "sel", [NST, NST * 128])
        o_y = dout(nc, "y", [ESL, TT])

        s = begin()
        setup_consts(nc, s, sb)
        psA = ps("psA", [128, 2048])
        psB = ps("psB", [128, 2048])

        sAlog = sb("sAlog", [128, 64])
        sA = sb("sA", [128, 64])
        ssel = sb("ssel", [NST, NST * 128])
        s.dma("sync", sAlog[:], Asl)
        s.dma("sync", ssel[:], sel)
        s.barrier()
        s.issue("scalar", lambda sc: sc.activation(out=sA[:], in_=sAlog[:], func=AF.Exp))
        s.barrier()
        s.issue("vector", lambda v: v.tensor_scalar_mul(sA[:], sA[:], -1.0))
        s.barrier()

        dch = [sb(f"dch{e}", [128, TC]) for e in range(2)]
        uch = [sb(f"uch{e}", [128, TC], BF16) for e in range(2)]
        duch = [sb(f"duch{e}", [128, TC]) for e in range(2)]
        bch = sb("bch", [NST, TC])
        cch = sb("cch", [NST, TC])
        Bb = sb("Bb", [128, 8 * TC])
        Cb = sb("Cb", [128, 8 * TC])
        dA = sb("dA", [128, 8 * TC])
        dbu = sb("dbu", [128, 8 * TC])
        hsc = sb("hsc", [128, 8 * TC])
        st = [sb(f"st{e}", [128, NST]) for e in range(2)]
        yac = [sb(f"yac{e}", [128, TC]) for e in range(2)]

        for half in range(2):
            for e in range(2):
                s.issue("vector", lambda v, e=e: v.memset(st[e][:], 0.0))
            s.barrier()
            for c in range(NCH):
                t0 = half * L + c * TC
                for e in range(2):
                    s.dma("sync", dch[e][:], dS[e*128:(e+1)*128, t0:t0+TC])
                    s.dma("sync", uch[e][:], uS[e*128:(e+1)*128, t0:t0+TC])
                s.dma("sync", bch[:], Bh[:, t0:t0+TC])
                s.dma("sync", cch[:], Ch[:, t0:t0+TC])
                s.barrier()
                for e in range(2):
                    s.issue("vector", lambda v, e=e: v.tensor_tensor(
                        duch[e][:], dch[e][:], uch[e][:], ALU.mult))
                s.barrier()
                for ng in range(4):
                    for rnd in range(2):
                        for i in range(4):
                            n = ng * 8 + rnd * 4 + i
                            s.issue("tensor", lambda te, n=n, i=i: te.matmul(
                                psA[:, i*512:(i+1)*512], ssel[:, n*128:(n+1)*128],
                                bch[:], start=True, stop=True))
                            s.issue("tensor", lambda te, n=n, i=i: te.matmul(
                                psB[:, i*512:(i+1)*512], ssel[:, n*128:(n+1)*128],
                                cch[:], start=True, stop=True))
                        s.barrier()
                        for i in range(4):
                            n8 = rnd * 4 + i
                            s.issue("vector", lambda v, n8=n8, i=i: v.tensor_copy(
                                Bb[:, n8*TC:(n8+1)*TC], psA[:, i*512:(i+1)*512]))
                            s.issue("vector", lambda v, n8=n8, i=i: v.tensor_copy(
                                Cb[:, n8*TC:(n8+1)*TC], psB[:, i*512:(i+1)*512]))
                        s.barrier()
                    for e in range(2):
                        for n8 in range(8):
                            n = ng * 8 + n8
                            s.issue("scalar", lambda sc, e=e, n=n, n8=n8: sc.activation(
                                out=dA[:, n8*TC:(n8+1)*TC], in_=dch[e][:], func=AF.Exp,
                                scale=sA[:, e*32+n:e*32+n+1]))
                        s.barrier()
                        s.issue("vector", lambda v, e=e: v.tensor_tensor(
                            out=dbu[:].rearrange("p (n t) -> p n t", n=8),
                            in0=duch[e][:, None, :].broadcast_to([128, 8, TC]),
                            in1=Bb[:].rearrange("p (n t) -> p n t", n=8),
                            op=ALU.mult))
                        s.barrier()
                        def scans(v, e=e, ng=ng):
                            last = None
                            for n8 in range(8):
                                n = ng * 8 + n8
                                last = v.tensor_tensor_scan(
                                    hsc[:, n8*TC:(n8+1)*TC], dA[:, n8*TC:(n8+1)*TC],
                                    dbu[:, n8*TC:(n8+1)*TC], st[e][:, n:n+1],
                                    ALU.mult, ALU.add)
                            return last
                        s.issue("vector", scans)
                        s.barrier()
                        s.issue("vector", lambda v, e=e, ng=ng: v.tensor_copy(
                            st[e][:, ng*8:(ng+1)*8],
                            hsc[:].rearrange("p (n t) -> p n t", n=8)[:, :, TC-1]))
                        s.barrier()
                        s.issue("vector", lambda v: v.tensor_tensor(
                            out=hsc[:].rearrange("p (n t) -> p n t", n=8),
                            in0=hsc[:].rearrange("p (n t) -> p n t", n=8),
                            in1=Cb[:].rearrange("p (n t) -> p n t", n=8), op=ALU.mult))
                        s.barrier()
                        s.issue("vector", lambda v: v.tensor_tensor(
                            out=hsc[:, 0:4*TC], in0=hsc[:, 0:4*TC], in1=hsc[:, 4*TC:8*TC], op=ALU.add))
                        s.barrier()
                        s.issue("vector", lambda v: v.tensor_tensor(
                            out=hsc[:, 0:2*TC], in0=hsc[:, 0:2*TC], in1=hsc[:, 2*TC:4*TC], op=ALU.add))
                        s.barrier()
                        s.issue("vector", lambda v: v.tensor_tensor(
                            out=hsc[:, 0:TC], in0=hsc[:, 0:TC], in1=hsc[:, TC:2*TC], op=ALU.add))
                        s.barrier()
                        if ng == 0:
                            s.issue("vector", lambda v, e=e: v.tensor_copy(yac[e][:], hsc[:, 0:TC]))
                        else:
                            s.issue("vector", lambda v, e=e: v.tensor_tensor(
                                yac[e][:], yac[e][:], hsc[:, 0:TC], ALU.add))
                        s.barrier()
                for e in range(2):
                    s.dma("sync", o_y[e*128:(e+1)*128, t0:t0+TC], yac[e][:])
                s.barrier()
    return nc


def _prep_beta(inputs, blk, delta_full, u_full, B_full, C_full):
    sel = np.zeros((NST, NST * 128), np.float32)
    for n in range(NST):
        sel[n, n*128:(n+1)*128] = 1.0
    maps = []
    Al = inputs["L0_attn_A_log"][blk]
    for c in range(NC):
        e0 = c * ESL
        Asl = np.ascontiguousarray(Al[e0:e0+ESL].reshape(2, 128, NST).transpose(1, 0, 2).reshape(128, 64))
        maps.append({"dS": np.ascontiguousarray(delta_full[e0:e0+ESL]),
                     "uS": np.ascontiguousarray(u_full[e0:e0+ESL]),
                     "Bh": B_full, "Ch": C_full, "Asl": Asl, "sel": sel, **CONSTS})
    return maps



def build_gamma():
    with program("gamma") as (nc, sb, ps, begin):
        yD = din(nc, "yD", [ED, TK])
        uD = din(nc, "uD", [ED, TK], BF16)
        zD = din(nc, "zD", [ED, TK], BF16)
        DpR = din(nc, "DpR", [128, 16])
        outwT = din(nc, "outwT", [ED, DM])
        hD = din(nc, "hD", [DM, TK])
        o_h = dout(nc, "h2", [DM, TK])

        s = begin()
        setup_consts(nc, s, sb)
        psA = ps("psA", [128, 2048])
        sdp = sb("sdp", [128, 16])
        s.dma("sync", sdp[:], DpR)
        swo = [sb(f"swo{k}", [128, DM], BF16) for k in range(16)]
        for k in range(16):
            s.dma("gpsimd", swo[k][:], outwT[k*128:(k+1)*128, :])
        yg = [sb(f"yg{e}", [128, TK], BF16) for e in range(16)]
        ybuf = sb("ybuf", [128, TK])
        ubuf = sb("ubuf", [128, TK], BF16)
        zbuf = sb("zbuf", [128, TK], BF16)
        s.barrier()
        for e in range(16):
            s.dma("sync", ybuf[:], yD[e*128:(e+1)*128, :])
            s.dma("sync", ubuf[:], uD[e*128:(e+1)*128, :])
            s.dma("sync", zbuf[:], zD[e*128:(e+1)*128, :])
            s.barrier()
            s.issue("vector", lambda v, e=e: v.scalar_tensor_tensor(
                out=ybuf[:], in0=ubuf[:], scalar=sdp[:, e:e+1], in1=ybuf[:],
                op0=ALU.mult, op1=ALU.add))
            s.barrier()
            s.issue("vector", lambda v, e=e: v.tensor_tensor(yg[e][:], ybuf[:], zbuf[:], ALU.mult))
            s.barrier()
        ht = sb("ht", [128, TK])
        for m in range(8):
            s.dma("sync", ht[:], hD[m*128:(m+1)*128, :])
            s.barrier()
            for f in range(4):
                def mm(tensor, m=m, f=f):
                    last = None
                    for k in range(16):
                        last = tensor.matmul(psA[:, (f%4)*512:(f%4+1)*512],
                                             swo[k][:, m*128:(m+1)*128],
                                             yg[k][:, f*512:(f+1)*512],
                                             start=(k == 0), stop=(k == 15))
                    return last
                s.issue("tensor", mm)
                s.barrier()
                s.issue("vector", lambda v, m=m, f=f: v.tensor_tensor(
                    ht[:, f*512:(f+1)*512], ht[:, f*512:(f+1)*512],
                    psA[:, (f%4)*512:(f%4+1)*512], ALU.add))
                s.barrier()
            s.dma("sync", o_h[m*128:(m+1)*128, :], ht[:])
            s.barrier()
    return nc


def _prep_gamma(inputs, blk, y_full, u_s_cores, z_s_cores, h_full):
    outwT = np.ascontiguousarray(inputs["L0_attn_out_w"][blk].T)
    DpR = np.ascontiguousarray(inputs["L0_attn_Dp"][blk].reshape(16, 128).T)
    maps = []
    for c in range(NC):
        b, q = c // 4, c % 4
        t0 = b * L + q * TK
        maps.append({"yD": np.ascontiguousarray(y_full[:, t0:t0+TK]),
                     "uD": u_s_cores[c], "zD": z_s_cores[c], "DpR": DpR,
                     "outwT": outwT,
                     "hD": np.ascontiguousarray(h_full[b, q*TK:(q+1)*TK, :].T),
                     **CONSTS})
    return maps


def build_fin():
    with program("fin") as (nc, sb, ps, begin):
        hD = din(nc, "hD", [DM, TK])
        fnG = din(nc, "fnG", [128, 8])
        fnB = din(nc, "fnB", [128, 8])
        o_s = dout(nc, "s_fm", [DM, TK], BF16)
        s = begin()
        setup_consts(nc, s, sb)
        psA = ps("psA", [128, 2048])
        psB = ps("psB", [128, 2048])
        ht = [sb(f"ht{k}", [128, TK]) for k in range(8)]
        for k in range(8):
            s.dma("sync", ht[k][:], hD[k*128:(k+1)*128, :])
        sg = sb("sg", [128, 8]); sbeta = sb("sbeta", [128, 8])
        s.dma("sync", sg[:], fnG); s.dma("sync", sbeta[:], fnB)
        s.barrier()
        r_b, mr_b = ln_stats(s, sb, psA, psB, ht, TK, "ln")
        outt = sb("outt", [128, TK], BF16)
        for k in range(8):
            s.issue("vector", lambda v, k=k: v.tensor_mul(ht[k][:], ht[k][:], r_b[:]))
            s.barrier()
            s.issue("vector", lambda v, k=k: v.tensor_sub(ht[k][:], ht[k][:], mr_b[:]))
            s.barrier()
            s.issue("vector", lambda v, k=k: v.tensor_scalar(
                out=outt[:], in0=ht[k][:], scalar1=sg[:, k:k+1], scalar2=sbeta[:, k:k+1],
                op0=ALU.mult, op1=ALU.add))
            s.barrier()
            s.dma("sync", o_s[k*128:(k+1)*128, :], outt[:])
            s.barrier()
    return nc



def _routing_top2(s, sb, lg, pr, pfA, pfB, t1, t2, iota8, nt):
    v3 = lambda x: x[:].rearrange("p (t e) -> p t e", e=8)
    mx = sb("rt_mx", [128, nt])
    sm = sb("rt_sm", [128, nt])
    w1 = sb("rt_w1", [128, nt]); w2 = sb("rt_w2", [128, nt])
    e1 = sb("rt_e1", [128, nt]); e2 = sb("rt_e2", [128, nt])
    oh1 = sb("rt_oh1", [128, nt * 8]); oh2 = sb("rt_oh2", [128, nt * 8])
    bc = lambda x: x[:, :, None].broadcast_to([128, nt, 8])
    s.issue("vector", lambda v: v.tensor_reduce(mx[:], v3(lg), AX.X, ALU.max))
    s.barrier()
    s.issue("vector", lambda v: v.tensor_tensor(v3(pr), v3(lg), bc(mx), ALU.subtract))
    s.barrier()
    s.issue("scalar", lambda sc: sc.activation(out=pr[:], in_=pr[:], func=AF.Exp))
    s.barrier()
    s.issue("vector", lambda v: v.tensor_reduce(sm[:], v3(pr), AX.X, ALU.add))
    s.barrier()
    s.issue("vector", lambda v: v.reciprocal(sm[:], sm[:]))
    s.barrier()
    s.issue("vector", lambda v: v.tensor_tensor(v3(pr), v3(pr), bc(sm), ALU.mult))
    s.barrier()
    def top1(prt, w, e, ohf):
        s.issue("vector", lambda v: v.tensor_reduce(w[:], v3(prt), AX.X, ALU.max))
        s.barrier()
        s.issue("vector", lambda v: v.tensor_tensor(v3(ohf), v3(prt), bc(w), ALU.is_ge))
        s.barrier()
        s.issue("vector", lambda v: v.tensor_copy(pfA[:], ohf[:]))
        s.barrier()
        for sh, (a, b) in zip((1, 2, 4), ((pfA, pfB), (pfB, pfA), (pfA, pfB))):
            s.issue("vector", lambda v, sh=sh, a=a, b=b: v.tensor_copy(
                v3(b)[:, :, 0:sh], v3(a)[:, :, 0:sh]))
            s.barrier()
            s.issue("vector", lambda v, sh=sh, a=a, b=b: v.tensor_tensor(
                v3(b)[:, :, sh:8], v3(a)[:, :, sh:8], v3(a)[:, :, 0:8-sh], ALU.add))
            s.barrier()
        s.issue("vector", lambda v: v.tensor_scalar(out=pfA[:], in0=pfB[:], scalar1=1.0,
                                                    scalar2=None, op0=ALU.is_le))
        s.barrier()
        s.issue("vector", lambda v: v.tensor_tensor(ohf[:], ohf[:], pfA[:], ALU.mult))
        s.barrier()
        s.issue("vector", lambda v: v.tensor_tensor(t1[:], ohf[:], iota8[:], ALU.mult))
        s.barrier()
        s.issue("vector", lambda v: v.tensor_reduce(e[:], v3(t1), AX.X, ALU.add))
        s.barrier()
    top1(pr, w1, e1, oh1)
    s.issue("vector", lambda v: v.scalar_tensor_tensor(out=pr[:], in0=oh1[:], scalar=-9.0,
                                                       in1=pr[:], op0=ALU.mult, op1=ALU.add))
    s.barrier()
    top1(pr, w2, e2, oh2)
    return {"w1": w1, "w2": w2, "e1": e1, "e2": e2, "oh1": oh1, "oh2": oh2}


def build_moe0a():
    NTT = TT // 128
    ECH = 4
    EC = 2 * TT // ECH
    with program("moe0a") as (nc, sb, ps, begin):
        xnF = din(nc, "xnF", [DM, TT], BF16)
        xnT = din(nc, "xnT", [TT, DM], BF16)
        wr = din(nc, "wr", [DM, 8])
        w1e = din(nc, "w1e", [DM, 4096])
        w2e = din(nc, "w2e", [4096, DM])
        myhot = din(nc, "myhot", [8, 1])
        identf = din(nc, "identf", [128, 128])
        identb = din(nc, "identb", [128, 128], BF16)
        iota8D = din(nc, "iota8", [128, NTT * 8])
        ctok = din(nc, "ctok", [128, 2 * ECH * EC // 128])
        gidxD = nc.dram_tensor("gidx", [CAP0, 2], F32, kind="Internal").ap()
        rowD = nc.dram_tensor("rowD", [1, 4096], F32, kind="Internal").ap()
        o_ye = dout(nc, "ye", [CAP0, DM], BF16)
        o_meta = dout(nc, "meta", [128, NTT * 8])

        s = begin()
        setup_consts(nc, s, sb)
        psA = ps("psA", [128, 2048])
        psT = ps("psT", [128, 1024], BF16)
        bcreg = {}
        def mk_regs(g):
            r1 = g.alloc_register("bc_slot"); g.reg_mov(r1, CAP0 - 1)
            r2 = g.alloc_register("bc_tok"); g.reg_mov(r2, TT - 1)
            bcreg["slot"], bcreg["tok"] = r1, r2
            return None
        s.issue("gpsimd", mk_regs, incs=0)

        SC = [sb(f"SC{i}", [128, 8192]) for i in range(2)]
        BB = [sb(f"BB{i}", [128, 4096], BF16) for i in range(8)]
        lg = sb("lg", [128, NTT * 8])
        pr = sb("pr", [128, NTT * 8])
        pfA = sb("pfA", [128, NTT * 8]); pfB = sb("pfB", [128, NTT * 8])
        t1s = sb("t1s", [128, NTT * 8]); t2s = sb("t2s", [128, NTT * 8])
        iota8 = sb("iota8s", [128, NTT * 8])
        sif = sb("sif", [128, 128]); sib = sb("sib", [128, 128], BF16)
        swr = sb("swr", [128, 8], BF16)
        smy = sb("smy", [8, 1])
        sctok = sb("sctok", [128, 2 * ECH * EC // 128])
        s.dma("sync", iota8[:], iota8D)
        s.dma("sync", sif[:], identf)
        s.dma("sync", sib[:], identb)
        s.dma("sync", smy[:], myhot)
        s.dma("sync", sctok[:], ctok)
        s.barrier()

        swrk = [sb(f"swrk{k}", [128, 8], BF16) for k in range(8)]
        for k in range(8):
            s.dma("gpsimd", swrk[k][:], wr[k*128:(k+1)*128, :])
        s.barrier()
        for ch in range(8):
            tc0 = ch * 2048
            for k in range(8):
                s.dma("sync", BB[k][:, 0:2048], xnF[k*128:(k+1)*128, tc0:tc0+2048])
            s.barrier()
            for tt in range(16):
                def mmlg(tensor, ch=ch, tt=tt):
                    last = None
                    for k in range(8):
                        last = tensor.matmul(psA[:, (tt % 4)*512:(tt % 4)*512+8],
                                             BB[k][:, tt*128:(tt+1)*128], swrk[k][:],
                                             start=(k == 0), stop=(k == 7))
                    return last
                s.issue("tensor", mmlg)
                if tt % 4 == 3:
                    s.barrier()
                    for t4 in range(tt-3, tt+1):
                        gt = ch * 16 + t4
                        s.issue("vector", lambda v, t4=t4, gt=gt: v.tensor_copy(
                            lg[:, gt*8:(gt+1)*8], psA[:, (t4 % 4)*512:(t4 % 4)*512+8]))
                    s.barrier()

        rt = _routing_top2(s, sb, lg, pr, pfA, pfB, t1s, t2s, iota8, NTT)

        oh_em = SC[0][0:8, :]
        rk_em = SC[1][0:8, :]
        carry = sb("carry", [8, 1])
        R8 = sb("R8", [1, 4096])
        mr128 = sb("mr128", [128, 64]); mo128 = sb("mo128", [128, 64])
        slotf = sb("slotf", [128, 64]); sloti = sb("sloti", [128, 64], I32)
        kp = sb("kp", [128, 64])
        r12t = sb("r12t", [128, NTT * 2])
        mbig = sb("mbig", [128, 64])
        s.issue("vector", lambda v: v.memset(mbig[:], 1e9))
        s.issue("vector", lambda v: v.memset(carry[:], 0.0))
        s.barrier()
        s.dma("sync", gidxD[0:CAP0, :].rearrange("(p c) x -> p (c x)", p=128), mbig[:, 0:64])
        s.barrier()
        for ch in range(ECH):
            tt0 = ch * 32
            for half in range(2):
                for i in range(16):
                    tt = tt0 + half * 16 + i
                    for sl, oh in ((0, rt["oh1"]), (1, rt["oh2"])):
                        slot = (2 * i + sl) % 8
                        s.issue("tensor", lambda te, tt=tt, slot=slot, oh=oh: te.transpose(
                            psA[0:8, slot*128:(slot+1)*128], oh[:, tt*8:(tt+1)*8], sif[:]))
                    if i % 4 == 3:
                        s.barrier()
                        for ii in range(i - 3, i + 1):
                            for sl in (0, 1):
                                slot = (2 * ii + sl) % 8
                                col0 = (half * 16 + ii) * 256 + sl
                                s.issue("vector", lambda v, slot=slot, col0=col0: v.tensor_copy(
                                    oh_em[:, col0:col0+255:2],
                                    psA[0:8, slot*128:(slot+1)*128]))
                        s.barrier()
            s.issue("vector", lambda v: v.tensor_tensor_scan(
                rk_em[:], oh_em[:], oh_em[:], carry[0:8, 0:1], ALU.add, ALU.bypass))
            s.barrier()
            s.issue("vector", lambda v: v.tensor_copy(carry[:], rk_em[:, 8191:8192]))
            s.barrier()
            s.issue("vector", lambda v: v.tensor_tensor(rk_em[:], rk_em[:], oh_em[:], ALU.subtract))
            s.barrier()
            for which, dst in ((0, mr128), (1, mo128)):
                srcT = rk_em if which == 0 else oh_em
                for hb in range(2):
                    for i in range(8):
                        ii = hb * 8 + i
                        s.issue("tensor", lambda te, i=i, ii=ii, srcT=srcT: te.matmul(
                            psA[0:1, (i % 4)*512:(i % 4)*512+512], smy[:, 0:1],
                            srcT[:, ii*512:(ii+1)*512], start=True, stop=True))
                        if i % 4 == 3:
                            s.barrier()
                            for i2 in range(i-3, i+1):
                                s.issue("vector", lambda v, i2=i2: v.tensor_copy(
                                    R8[:, i2*512:(i2+1)*512], psA[0:1, (i2 % 4)*512:(i2 % 4)*512+512]))
                            s.barrier()
                    s.dma("sync", rowD, R8[:])
                    s.barrier()
                    s.dma("sync", dst[:, hb*32:(hb+1)*32],
                          rowD.rearrange("x (p c) -> (x p) c", p=128))
                    s.barrier()
            s.issue("vector", lambda v: v.tensor_scalar(out=kp[:], in0=mr128[:], scalar1=float(CAP0),
                                                        scalar2=None, op0=ALU.is_lt))
            s.barrier()
            s.issue("vector", lambda v: v.tensor_tensor(kp[:], kp[:], mo128[:], ALU.mult))
            s.barrier()
            s.issue("vector", lambda v: v.tensor_scalar(out=slotf[:], in0=kp[:], scalar1=-1e9,
                                                        scalar2=1e9, op0=ALU.mult, op1=ALU.add))
            s.barrier()
            s.issue("vector", lambda v: v.scalar_tensor_tensor(
                out=slotf[:], in0=mr128[:], scalar=1.0, in1=slotf[:]
                , op0=ALU.mult, op1=ALU.add) if False else v.tensor_tensor(
                kp[:], mr128[:], kp[:], ALU.mult))
            s.barrier()
            s.issue("vector", lambda v: v.tensor_tensor(slotf[:], slotf[:], kp[:], ALU.add))
            s.barrier()
            s.issue("vector", lambda v: v.tensor_copy(sloti[:], slotf[:]))
            s.barrier()
            for j in range(64):
                s.issue("gpsimd", lambda g, ch=ch, j=j: g.indirect_dma_start(
                    out=gidxD, out_offset=IndirectOffsetOnAxis(sloti[:, j:j+1], 0),
                    in_=sctok[:, (ch*64+j)*2:(ch*64+j)*2+2], in_offset=None,
                    bounds_check=bcreg["slot"], oob_is_err=False), incs=16)
                if j % 4 == 3:
                    s.barrier()
            s.barrier()
            s.issue("vector", lambda v: v.tensor_tensor(rk_em[:], rk_em[:], oh_em[:], ALU.mult))
            s.barrier()
            for hb in range(2):
                for i in range(8):
                    ii = hb * 8 + i
                    s.issue("tensor", lambda te, i=i, ii=ii: te.matmul(
                        psA[0:1, (i % 4)*512:(i % 4)*512+512], s.ones_col[0:8, 0:1],
                        rk_em[:, ii*512:(ii+1)*512], start=True, stop=True))
                    if i % 4 == 3:
                        s.barrier()
                        for i2 in range(i-3, i+1):
                            s.issue("vector", lambda v, i2=i2: v.tensor_copy(
                                R8[:, i2*512:(i2+1)*512], psA[0:1, (i2 % 4)*512:(i2 % 4)*512+512]))
                        s.barrier()
                s.dma("sync", rowD, R8[:])
                s.barrier()
                s.dma("sync", r12t[:, (tt0 + hb*16)*2:(tt0 + hb*16 + 16)*2],
                      rowD.rearrange("x (t p k) -> (x p) t k", p=128, k=2))
                s.barrier()

        for i, tname in enumerate(("e1", "e2", "w1", "w2")):
            s.dma("sync", o_meta[:, i*NTT:(i+1)*NTT], rt[tname][:])
        s.dma("sync", o_meta[:, 4*NTT:6*NTT], r12t[:])
        s.barrier()

        s.dma("sync", o_meta[:, 832:896], slotf[:])
        s.dma("sync", o_meta[:, 896:960], mo128[:])
        s.dma("sync", o_meta[:, 960:1024], mr128[:])
        s.barrier()
        for k in range(8):
            s.dma("gpsimd", BB[k][:], w1e[k*128:(k+1)*128, :])
        gidxs = sb("gidxs", [128, 64]); gidxi = sb("gidxi", [128, 64], I32)
        s.dma("sync", gidxs[:].rearrange("p (j x) -> p j x", x=2),
              gidxD[0:CAP0, :].rearrange("(j p) x -> p j x", p=128))
        s.barrier()
        s.issue("vector", lambda v: v.tensor_copy(gidxi[:], gidxs[:]))
        s.barrier()
        s.dma("sync", o_meta[:, 768:832], gidxs[:, 0:64:1][:, 0:64][:, ::1][:, 0:64][:, 0:64] if False else gidxs[:, 0:64])
        s.barrier()
        xg, yfm, xf = [], [], []
        for tsr in (pr, pfA, t1s, t2s):
            bv = tsr[:].bitcast(BF16)
            xg.append(bv[:, 0:1024]); xg.append(bv[:, 1024:2048])
        for tsr in (lg, iota8):
            bv = tsr[:].bitcast(BF16)
            yfm.append(bv[:, 0:1024]); yfm.append(bv[:, 1024:2048])
        for tsr in (rt["oh1"], rt["oh2"]):
            bv = tsr[:].bitcast(BF16)
            yfm.append(bv[:, 0:1024]); yfm.append(bv[:, 1024:2048])
        xf = [sb(f"xf{k}", [128, 1024], BF16) for k in range(8)]
        _pfBv = pfB[:].bitcast(BF16)
        w2b = [_pfBv[:, k*128:(k+1)*128] for k in range(16)]
        h1v = []
        for i in range(2):
            bv = SC[i][:].bitcast(BF16)
            for j in range(16):
                h1v.append(bv[:, j*1024:(j+1)*1024])
        for sc in range(4):
            for g in range(8):
                s.issue("vector", lambda v, g=g: v.memset(xg[g][:], 0.0))
            s.barrier()
            for g in range(8):
                s.issue("gpsimd", lambda gp, sc=sc, g=g: gp.indirect_dma_start(
                    out=xg[g][:], out_offset=None, in_=xnT,
                    in_offset=IndirectOffsetOnAxis(gidxi[:, (sc*8+g)*2:(sc*8+g)*2+1], 0),
                    bounds_check=bcreg["tok"], oob_is_err=False), incs=16)
                if g % 4 == 3:
                    s.barrier()
            s.barrier()
            for g in range(8):
                for k in range(8):
                    s.issue("tensor", lambda te, g=g, k=k: te.transpose(
                        psT[:, (k % 8)*128:(k % 8 + 1)*128], xg[g][:, k*128:(k+1)*128], sib[:]))
                s.barrier()
                for k in range(8):
                    s.issue("vector", lambda v, g=g, k=k: v.tensor_copy(
                        xf[k][:, g*128:(g+1)*128], psT[:, (k % 8)*128:(k % 8 + 1)*128]))
                s.barrier()
            for m in range(32):
                for f in range(2):
                    def mmh(tensor, m=m, f=f):
                        last = None
                        for k in range(8):
                            last = tensor.matmul(psA[:, (f % 4)*512:(f % 4 + 1)*512],
                                                 BB[k][:, m*128:(m+1)*128],
                                                 xf[k][:, f*512:(f+1)*512],
                                                 start=(k == 0), stop=(k == 7))
                        return last
                    s.issue("tensor", mmh)
                s.barrier()
                for f in range(2):
                    s.issue("scalar", lambda scl, m=m, f=f: scl.activation(
                        out=h1v[m][:, f*512:(f+1)*512], in_=psA[:, (f % 4)*512:(f % 4 + 1)*512],
                        func=AF.Gelu_apprx_tanh))
                s.barrier()
            for m in range(8):
                for rnd in range(2):
                    for kb in range(16):
                        s.dma("gpsimd", w2b[kb], w2e[(rnd*16+kb)*128:(rnd*16+kb+1)*128,
                                                     m*128:(m+1)*128])
                    s.barrier()
                    for f in range(2):
                        def mmy(tensor, m=m, f=f, rnd=rnd):
                            last = None
                            for kb in range(16):
                                last = tensor.matmul(psA[:, (f % 4)*512:(f % 4 + 1)*512],
                                                     w2b[kb], h1v[rnd*16+kb][:, f*512:(f+1)*512],
                                                     start=(rnd == 0 and kb == 0),
                                                     stop=(rnd == 1 and kb == 15))
                            return last
                        s.issue("tensor", mmy)
                    s.barrier()
                s.issue("vector", lambda v, m=m: v.tensor_copy(yfm[m][:, 0:512], psA[:, 0:512]))
                s.issue("vector", lambda v, m=m: v.tensor_copy(yfm[m][:, 512:1024], psA[:, 512:1024]))
                s.barrier()
            for g in range(8):
                for k in range(8):
                    s.issue("tensor", lambda te, g=g, k=k: te.transpose(
                        psT[:, (k % 8)*128:(k % 8 + 1)*128], yfm[k][:, g*128:(g+1)*128], sib[:]))
                s.barrier()
                for k in range(8):
                    s.issue("vector", lambda v, g=g, k=k: v.tensor_copy(
                        xg[g][:, k*128:(k+1)*128], psT[:, (k % 8)*128:(k % 8 + 1)*128]))
                s.barrier()
                s.dma("sync", o_ye[sc*1024+g*128:sc*1024+(g+1)*128, :], xg[g][:])
            s.barrier()
    return nc


def _prep_moe0a(inputs, xn_cores):
    xnF = _cores_to_ett(xn_cores, BF)
    xnT = np.ascontiguousarray(xnF.T)
    iota8 = np.tile(np.arange(8, dtype=np.float32), (128, TT // 128))
    ctok = ((np.arange(2 * TT) // 2).astype(np.float32)
            .reshape(4, 2, 128, 32).transpose(2, 0, 1, 3).reshape(128, 256))
    ctok = np.repeat(ctok, 2, axis=1)
    identf = np.eye(128, dtype=np.float32)
    identb = np.eye(128, dtype=ml_dtypes.bfloat16)
    maps = []
    for c in range(NC):
        myhot = np.zeros((8, 1), np.float32); myhot[c] = 1.0
        maps.append({"xnF": xnF, "xnT": xnT, "wr": inputs["L0_ffn_w_router"],
                     "w1e": np.ascontiguousarray(inputs["L0_ffn_w1"][c]),
                     "w2e": np.ascontiguousarray(inputs["L0_ffn_w2"][c]),
                     "myhot": myhot, "identf": identf, "identb": identb,
                     "iota8": iota8, "ctok": ctok, **CONSTS})
    return maps


def build_moe0b():
    with program("moe0b") as (nc, sb, ps, begin):
        yAll = din(nc, "yAll", [NE * CAP0, DM], BF16)
        meta6 = din(nc, "meta6", [128, 96])
        biasR = din(nc, "biasR", [128, DM])
        sF = din(nc, "sF", [DM, TK], BF16)
        xT = din(nc, "xT", [DM, TK])
        identb = din(nc, "identb", [128, 128], BF16)
        o_x1 = dout(nc, "x1", [DM, TK])
        o_g = dout(nc, "g_fm", [DM, TK], BF16)

        s = begin()
        setup_consts(nc, s, sb)
        psA = ps("psA", [128, 2048])
        psT = ps("psT", [128, 1024], BF16)
        bcreg = {}
        def mk_regs(g):
            r = g.alloc_register("bc_y"); g.reg_mov(r, NE * CAP0 - 1)
            bcreg["y"] = r
            return None
        s.issue("gpsimd", mk_regs, incs=0)
        sm = sb("sm", [128, 96])
        sbias = sb("sbias", [128, DM])
        sib = sb("sib", [128, 128], BF16)
        s.dma("sync", sm[:], meta6); s.dma("sync", sbias[:], biasR)
        s.dma("sync", sib[:], identb)
        s.barrier()
        e1 = sm[:, 0:16]; e2 = sm[:, 16:32]; w1 = sm[:, 32:48]; w2 = sm[:, 48:64]
        ff = [sb(f"ff{i}", [128, 16]) for i in range(2)]
        kpt = sb("kpt", [128, 16])
        idx = [sb(f"idx{i}", [128, 16], I32) for i in range(2)]
        for i, (ee, rsl) in enumerate(((e1, 0), (e2, 1))):
            rr = sm[:, 64 + rsl:96:2]
            s.issue("vector", lambda v, i=i, ee=ee, rr=rr: v.scalar_tensor_tensor(
                out=ff[i][:], in0=ee, scalar=float(CAP0), in1=rr, op0=ALU.mult, op1=ALU.add))
            s.barrier()
            s.issue("vector", lambda v, i=i, rr=rr: v.tensor_scalar(
                out=kpt[:], in0=rr, scalar1=float(CAP0), scalar2=None, op0=ALU.is_lt))
            s.barrier()
            s.issue("vector", lambda v, i=i: v.tensor_tensor(ff[i][:], ff[i][:], kpt[:], ALU.mult))
            s.barrier()
            s.issue("vector", lambda v, i=i: v.tensor_scalar(
                out=kpt[:], in0=kpt[:], scalar1=-1e9, scalar2=1e9, op0=ALU.mult, op1=ALU.add))
            s.barrier()
            s.issue("vector", lambda v, i=i: v.tensor_tensor(ff[i][:], ff[i][:], kpt[:], ALU.add))
            s.barrier()
            s.issue("vector", lambda v, i=i: v.tensor_copy(idx[i][:], ff[i][:]))
            s.barrier()
        y1g = sb("y1g", [128, DM], BF16)
        y2g = sb("y2g", [128, DM], BF16)
        gtm = sb("gtm", [128, DM])
        gfm = [sb(f"gfm{k}", [128, TK], BF16) for k in range(8)]
        for tt in range(16):
            s.issue("vector", lambda v: v.memset(y1g[:], 0.0))
            s.issue("vector", lambda v: v.memset(y2g[:], 0.0))
            s.barrier()
            s.issue("gpsimd", lambda g, tt=tt: g.indirect_dma_start(
                out=y1g[:], out_offset=None, in_=yAll,
                in_offset=IndirectOffsetOnAxis(idx[0][:, tt:tt+1], 0),
                bounds_check=bcreg["y"], oob_is_err=False), incs=16)
            s.issue("gpsimd", lambda g, tt=tt: g.indirect_dma_start(
                out=y2g[:], out_offset=None, in_=yAll,
                in_offset=IndirectOffsetOnAxis(idx[1][:, tt:tt+1], 0),
                bounds_check=bcreg["y"], oob_is_err=False), incs=16)
            s.barrier()
            s.issue("vector", lambda v, tt=tt: v.tensor_scalar(
                out=gtm[:], in0=y1g[:], scalar1=w1[:, tt:tt+1], scalar2=None, op0=ALU.mult))
            s.barrier()
            s.issue("vector", lambda v, tt=tt: v.scalar_tensor_tensor(
                out=gtm[:], in0=y2g[:], scalar=w2[:, tt:tt+1], in1=gtm[:],
                op0=ALU.mult, op1=ALU.add))
            s.barrier()
            s.issue("vector", lambda v: v.tensor_tensor(gtm[:], gtm[:], sbias[:], ALU.add))
            s.barrier()
            s.issue("scalar", lambda sc: sc.activation(out=gtm[:], in_=gtm[:], func=AF.Silu))
            s.barrier()
            s.issue("vector", lambda v: v.tensor_copy(y1g[:], gtm[:]))
            s.barrier()
            for k in range(8):
                s.issue("tensor", lambda te, k=k: te.transpose(
                    psT[:, (k % 8)*128:(k % 8 + 1)*128], y1g[:, k*128:(k+1)*128], sib[:]))
            s.barrier()
            for k in range(8):
                s.issue("vector", lambda v, k=k, tt=tt: v.tensor_copy(
                    gfm[k][:, tt*128:(tt+1)*128], psT[:, (k % 8)*128:(k % 8 + 1)*128]))
            s.barrier()
        xk = sb("xk", [128, TK])
        sk = sb("sk", [128, TK], BF16)
        for k in range(8):
            s.dma("sync", xk[:], xT[k*128:(k+1)*128, :])
            s.dma("sync", sk[:], sF[k*128:(k+1)*128, :])
            s.dma("sync", o_g[k*128:(k+1)*128, :], gfm[k][:])
            s.barrier()
            s.issue("vector", lambda v, k=k: v.tensor_mul(gfm[k][:], sk[:], gfm[k][:]))
            s.barrier()
            s.issue("vector", lambda v, k=k: v.tensor_tensor(xk[:], xk[:], gfm[k][:], ALU.add))
            s.barrier()
            s.dma("sync", o_x1[k*128:(k+1)*128, :], xk[:])
            s.barrier()
    return nc


def _prep_moe0b(inputs, meta_full, s_cores, x_input):
    biasR = np.broadcast_to(inputs["L0_ffn_bias"].reshape(1, DM), (128, DM)).copy()
    identb = np.eye(128, dtype=ml_dtypes.bfloat16)
    NTT = TT // 128
    maps = []
    for c in range(NC):
        b, q = c // 4, c % 4
        tt0 = (b * L + q * TK) // 128
        cols = []
        for i in range(4):
            cols.append(meta_full[:, i*NTT + tt0: i*NTT + tt0 + 16])
        cols.append(meta_full[:, 4*NTT + 2*tt0: 4*NTT + 2*(tt0 + 16)])
        meta6 = np.ascontiguousarray(np.concatenate(cols, axis=1))
        xT = np.ascontiguousarray(x_input[b, q*TK:(q+1)*TK, :].T)
        maps.append({"yAll": None, "meta6": meta6, "biasR": biasR, "sF": s_cores[c],
                     "xT": xT, "identb": identb, **CONSTS})
    return maps



def build_att1():
    with program("att1") as (nc, sb, ps, begin):
        x1D = din(nc, "x1D", [DM, TK])
        cosF = din(nc, "cosF", [DM, TK])
        sinF = din(nc, "sinF", [DM, TK])
        ipT = din(nc, "ipT", [DM, 3 * DM])
        ipB = din(nc, "ipB", [128, 24])
        o_xn1 = dout(nc, "xn1", [DM, TK])
        o_xn1b = dout(nc, "xn1b", [DM, TK], BF16)
        o_q = dout(nc, "q", [DM, TK], BF16)
        o_k = dout(nc, "k", [DM, TK], BF16)
        o_v = dout(nc, "v", [DM, TK], BF16)

        s = begin()
        setup_consts(nc, s, sb)
        psA = ps("psA", [128, 2048])
        psB = ps("psB", [128, 2048])
        xt = [sb(f"x{k}", [128, TK]) for k in range(8)]
        for k in range(8):
            s.dma("sync", xt[k][:], x1D[k*128:(k+1)*128, :])
        sbi = sb("sbi", [128, 24])
        s.dma("sync", sbi[:], ipB)
        s.barrier()
        r_b, mr_b = ln_stats(s, sb, psA, psB, xt, TK, "ln")
        for k in range(8):
            s.issue("vector", lambda v, k=k: v.tensor_mul(xt[k][:], xt[k][:], r_b[:]))
        s.barrier()
        for k in range(8):
            s.issue("vector", lambda v, k=k: v.tensor_sub(xt[k][:], xt[k][:], mr_b[:]))
        s.barrier()
        xnbt = sb("xnbt", [128, TK], BF16)
        for k in range(8):
            s.dma("sync", o_xn1[k*128:(k+1)*128, :], xt[k][:])
            s.issue("vector", lambda v, k=k: v.tensor_copy(xnbt[:], xt[k][:]))
            s.barrier()
            s.dma("sync", o_xn1b[k*128:(k+1)*128, :], xnbt[:])
            s.barrier()
        SWAPM = [(i ^ 1) for i in range(32)]
        ct = r_b; st_ = mr_b
        xrb = [sb(f"xrb{k}", [128, TK], BF16) for k in range(8)]
        xsw = sb("xsw", [128, TK])
        for k in range(8):
            s.dma("sync", ct[:], cosF[k*128:(k+1)*128, :])
            s.dma("sync", st_[:], sinF[k*128:(k+1)*128, :])
            s.barrier()
            s.issue("vector", lambda v, k=k: v.stream_shuffle(xsw[:], xt[k][:], SWAPM))
            s.barrier()
            s.issue("vector", lambda v, k=k: v.tensor_mul(xsw[:], xsw[:], st_[:]))
            s.barrier()
            s.issue("vector", lambda v, k=k: v.tensor_mul(xt[k][:], xt[k][:], ct[:]))
            s.barrier()
            s.issue("vector", lambda v, k=k: v.tensor_add(xt[k][:], xt[k][:], xsw[:]))
            s.barrier()
            s.issue("vector", lambda v, k=k: v.tensor_copy(xrb[k][:], xt[k][:]))
            s.barrier()
        wb = [[sb(f"wq{pp}_{k}", [128, 128], BF16) for k in range(8)] for pp in range(2)]
        orow = sb("orow", [128, TK], BF16)
        for k in range(8):
            s.dma("gpsimd", wb[0][k][:], ipT[k*128:(k+1)*128, 0:128])
        s.barrier()
        for m in range(24):
            pp = m % 2
            if m + 1 < 24:
                for k in range(8):
                    s.dma("gpsimd", wb[1 - pp][k][:], ipT[k*128:(k+1)*128, (m+1)*128:(m+2)*128])
            for f in range(4):
                def mm(tensor, pp=pp, f=f):
                    last = None
                    for k in range(8):
                        last = tensor.matmul(psA[:, (f % 4)*512:(f % 4 + 1)*512], wb[pp][k][:],
                                             xrb[k][:, f*512:(f+1)*512],
                                             start=(k == 0), stop=(k == 7))
                    return last
                s.issue("tensor", mm)
            s.barrier()
            for f in range(4):
                s.issue("scalar", lambda sc, m=m, f=f: sc.activation(
                    out=orow[:, f*512:(f+1)*512], in_=psA[:, (f % 4)*512:(f % 4 + 1)*512],
                    func=AF.Identity, bias=sbi[:, m:m+1]))
            s.barrier()
            dst = (o_q, o_k, o_v)[m // 8]
            mm8 = m % 8
            s.dma("sync", dst[mm8*128:(mm8+1)*128, :], orow[:])
            s.barrier()
    return nc


def build_att2():
    KW = TK + 256
    NG = 32
    with program("att2") as (nc, sb, ps, begin):
        qD = din(nc, "qD", [DM, TK], BF16)
        keD = din(nc, "keD", [DM, KW], BF16)
        veD = din(nc, "veD", [DM, KW], BF16)
        kgD = din(nc, "kgD", [DM, NG], BF16)
        vgD = din(nc, "vgD", [DM, NG], BF16)
        maskD = din(nc, "maskD", [128, 16 * 448])
        owT = din(nc, "owT", [DM, DM])
        owB = din(nc, "owB", [128, 8])
        identb = din(nc, "identb", [128, 128], BF16)
        o_s1 = dout(nc, "s1", [DM, TK], BF16)

        s = begin()
        setup_consts(nc, s, sb)
        psA = ps("psA", [128, 1024])
        psT = ps("psT", [128, 1024], BF16)
        pO = ps("pO", [64, 512])
        qt_ = [sb(f"q{k}", [128, TK], BF16) for k in range(8)]
        ke = [sb(f"ke{k}", [128, KW], BF16) for k in range(8)]
        ve = [sb(f"ve{k}", [128, KW], BF16) for k in range(8)]
        kg = sb("kg", [128, 8 * NG], BF16)
        vg = sb("vg", [128, 8 * NG], BF16)
        sib = sb("sib", [128, 128], BF16)
        sob = sb("sob", [128, 8])
        for k in range(8):
            s.dma("sync", qt_[k][:], qD[k*128:(k+1)*128, :])
            s.dma("sync", ke[k][:], keD[k*128:(k+1)*128, :])
            s.dma("sync", ve[k][:], veD[k*128:(k+1)*128, :])
            s.dma("sync", kg[:, k*NG:(k+1)*NG], kgD[k*128:(k+1)*128, :])
            s.dma("sync", vg[:, k*NG:(k+1)*NG], vgD[k*128:(k+1)*128, :])
        s.dma("sync", sib[:], identb)
        s.dma("sync", sob[:], owB)
        s.barrier()
        for k in range(8):
            s.issue("vector", lambda v, k=k: v.tensor_scalar_mul(qt_[k][:], qt_[k][:], 0.125))
        s.barrier()
        mrow = sb("mrow", [128, 16 * 448])
        s.dma("sync", mrow[:], maskD)
        s.barrier()
        ot = [sb(f"ot{k}", [128, TK], BF16) for k in range(8)]
        sc_ = sb("sc_", [128, 448])
        scb = sb("scb", [128, 448], BF16)
        nmx = sb("nmx", [128, 1])
        rsum = sb("rsum", [128, 1])
        vT = [sb(f"vT{j}", [128, 64], BF16) for j in range(18)]
        vTg = sb("vTg", [32, 64], BF16)
        aT = [sb(f"aT{j}", [128, 128], BF16) for j in range(4)]
        for h in range(16):
            dk = h // 2
            r0 = (h % 2) * 64
            for j in range(18):
                s.issue("tensor", lambda te, j=j, dk=dk, r0=r0: te.transpose(
                    psT[:, (j % 4)*128:(j % 4)*128+64],
                    ve[dk][r0:r0+64, j*128:(j+1)*128], sib[r0:r0+64, r0:r0+64]))
                if j % 4 == 3 or j == 17:
                    s.barrier()
                    for jj in range(j - (j % 4), j + 1):
                        s.issue("vector", lambda v, jj=jj: v.tensor_copy(
                            vT[jj][:], psT[:, (jj % 4)*128:(jj % 4)*128+64]))
                    s.barrier()
            s.issue("tensor", lambda te, dk=dk, r0=r0: te.transpose(
                psT[0:32, 0:64], vg[r0:r0+64, dk*NG:(dk+1)*NG], sib[r0:r0+64, r0:r0+64]))
            s.barrier()
            s.issue("vector", lambda v: v.tensor_copy(vTg[:], psT[0:32, 0:64]))
            s.barrier()
            for qt in range(16):
                def mmsc(tensor, qt=qt, dk=dk, r0=r0):
                    tensor.matmul(psA[:, 0:384], qt_[dk][r0:r0+64, qt*128:(qt+1)*128],
                                  ke[dk][r0:r0+64, qt*128:qt*128+384], start=True, stop=True)
                    return tensor.matmul(psA[:, 512:544],
                                         qt_[dk][r0:r0+64, qt*128:(qt+1)*128],
                                         kg[r0:r0+64, dk*NG:(dk+1)*NG], start=True, stop=True)
                s.issue("tensor", mmsc)
                s.barrier()
                s.issue("vector", lambda v, qt=qt: v.tensor_tensor(
                    sc_[:, 0:384], psA[:, 0:384], mrow[:, qt*448:qt*448+384], ALU.add))
                s.barrier()
                s.issue("vector", lambda v, qt=qt: v.tensor_tensor(
                    sc_[:, 384:416], psA[:, 512:544], mrow[:, qt*448+384:qt*448+416], ALU.add))
                s.barrier()
                s.issue("vector", lambda v: v.tensor_reduce(nmx[:], sc_[:, 0:416], AX.X, ALU.max))
                s.barrier()
                s.issue("vector", lambda v: v.tensor_scalar_mul(nmx[:], nmx[:], -1.0))
                s.barrier()
                s.issue("scalar", lambda scl: scl.activation(
                    out=sc_[:, 0:416], in_=sc_[:, 0:416], func=AF.Exp, bias=nmx[:, 0:1],
                    accum_out=rsum[:]))
                s.barrier()
                s.issue("vector", lambda v: v.reciprocal(rsum[:], rsum[:]))
                s.barrier()
                s.issue("vector", lambda v: v.tensor_scalar(
                    out=scb[:, 0:416], in0=sc_[:, 0:416], scalar1=rsum[:, 0:1],
                    scalar2=None, op0=ALU.mult))
                s.barrier()
                for j in range(3):
                    s.issue("tensor", lambda te, j=j: te.transpose(
                        psT[:, (j % 4)*128:(j % 4 + 1)*128], scb[:, j*128:(j+1)*128], sib[:]))
                s.issue("tensor", lambda te: te.transpose(
                    psT[0:32, 3*128:3*128+128], scb[:, 384:416], sib[:]))
                s.barrier()
                for j in range(3):
                    s.issue("vector", lambda v, j=j: v.tensor_copy(
                        aT[j][:], psT[:, (j % 4)*128:(j % 4 + 1)*128]))
                s.issue("vector", lambda v: v.tensor_copy(aT[3][0:32, :], psT[0:32, 3*128:3*128+128]))
                s.barrier()
                def mmav(tensor, qt=qt):
                    for j in range(3):
                        tensor.matmul(pO[:, 0:128], vT[qt + j][:], aT[j][:],
                                      start=(j == 0), stop=False)
                    return tensor.matmul(pO[:, 0:128], vTg[:], aT[3][0:32, :],
                                         start=False, stop=True)
                s.issue("tensor", mmav)
                s.barrier()
                s.issue("vector", lambda v, qt=qt, dk=dk, r0=r0: v.tensor_copy(
                    ot[dk][r0:r0+64, qt*128:(qt+1)*128], pO[:, 0:128]))
                s.barrier()
        wb = [[sb(f"wo{pp}_{k}", [128, 128], BF16) for k in range(8)] for pp in range(2)]
        orow = sb("orow", [128, TK], BF16)
        for k in range(8):
            s.dma("gpsimd", wb[0][k][:], owT[k*128:(k+1)*128, 0:128])
        s.barrier()
        for m in range(8):
            pp = m % 2
            if m + 1 < 8:
                for k in range(8):
                    s.dma("gpsimd", wb[1 - pp][k][:], owT[k*128:(k+1)*128, (m+1)*128:(m+2)*128])
            for f in range(4):
                def mmo(tensor, pp=pp, f=f):
                    last = None
                    for k in range(8):
                        last = tensor.matmul(psA[:, 0:512], wb[pp][k][:],
                                             ot[k][:, f*512:(f+1)*512],
                                             start=(k == 0), stop=(k == 7))
                    return last
                s.issue("tensor", mmo)
                s.barrier()
                s.issue("scalar", lambda sc2, m=m, f=f: sc2.activation(
                    out=orow[:, f*512:(f+1)*512], in_=psA[:, 0:512],
                    func=AF.Identity, bias=sob[:, m:m+1]))
                s.barrier()
            s.dma("sync", o_s1[m*128:(m+1)*128, :], orow[:])
            s.barrier()
    return nc


def _rope_tables():
    HD = 64
    inv_freq = 1.0 / (10000.0 ** (np.arange(0, HD, 2, dtype=np.float32) / HD))
    pos = np.arange(L, dtype=np.float32)
    freqs = pos[:, None] * inv_freq[None, :]
    emb = np.concatenate([freqs, freqs], axis=-1)
    cos = np.cos(emb[:, ::2])
    sin = np.sin(emb[:, ::2])
    cosF = np.empty((DM, L), np.float32)
    sinF = np.empty((DM, L), np.float32)
    for hh in range(16):
        for i in range(32):
            cosF[hh*64 + 2*i] = cos[:, i]
            cosF[hh*64 + 2*i + 1] = cos[:, i]
            sinF[hh*64 + 2*i] = -sin[:, i]
            sinF[hh*64 + 2*i + 1] = sin[:, i]
    return cosF, sinF


def _prep_att1(inputs, x1_cores):
    cosF, sinF = _rope_tables()
    ipT = np.ascontiguousarray(inputs["L1_attn_in_proj_w"].T)
    ipB = np.ascontiguousarray(inputs["L1_attn_in_proj_b"].reshape(24, 128).T)
    maps = []
    for c in range(NC):
        b, q = c // 4, c % 4
        maps.append({"x1D": np.ascontiguousarray(x1_cores[c]),
                     "cosF": np.ascontiguousarray(cosF[:, q*TK:(q+1)*TK]),
                     "sinF": np.ascontiguousarray(sinF[:, q*TK:(q+1)*TK]),
                     "ipT": ipT, "ipB": ipB, **CONSTS})
    return maps


def _att_masks():
    W, GS = 128, 256
    masks = []
    for c in range(NC):
        q = c % 4
        m = np.zeros((128, 16 * 448), np.float32)
        for qt in range(16):
            iabs = (q * TK + qt * 128) + np.arange(128)[:, None]
            jw = (q * TK + qt * 128 - 128) + np.arange(384)[None, :]
            ok = (np.abs(iabs - jw) <= W) | ((jw % GS == 0) & (jw >= 0))
            ok &= (jw >= 0) & (jw < L)
            m[:, qt*448:qt*448+384] = np.where(ok, 0.0, -1e30)
            jg = np.arange(32)[None, :] * GS
            okg = (np.abs(iabs - jg) > W)
            okg &= ~((jg >= q * TK + qt * 128 - 128) & (jg < q * TK + qt * 128 + 256))
            m[:, qt*448+384:qt*448+416] = np.where(okg, 0.0, -1e30)
        masks.append(m)
    return masks


def _prep_att2(inputs, q_cores, k_cores, v_cores):
    owT = np.ascontiguousarray(inputs["L1_attn_out_w"].T)
    owB = np.ascontiguousarray(inputs["L1_attn_out_b"].reshape(8, 128).T)
    identb = np.eye(128, dtype=ml_dtypes.bfloat16)
    masks = _att_masks()
    Z = np.zeros((DM, 128), ml_dtypes.bfloat16)
    maps = []
    for c in range(NC):
        b, q = c // 4, c % 4
        def ext(cores):
            left = cores[c - 1][:, -128:] if q > 0 else Z
            right = cores[c + 1][:, :128] if q < 3 else Z
            return np.ascontiguousarray(np.concatenate([left, cores[c], right], axis=1))
        kcat = np.concatenate([k_cores[4*b + i] for i in range(4)], axis=1)
        vcat = np.concatenate([v_cores[4*b + i] for i in range(4)], axis=1)
        maps.append({"qD": q_cores[c], "keD": ext(k_cores), "veD": ext(v_cores),
                     "kgD": np.ascontiguousarray(kcat[:, ::256]),
                     "vgD": np.ascontiguousarray(vcat[:, ::256]),
                     "maskD": masks[c], "owT": owT, "owB": owB, "identb": identb,
                     **CONSTS})
    return maps



def build_moe1a():
    NTT = TT // 128
    ECH = 2
    EC = TT // ECH
    with program("moe1a") as (nc, sb, ps, begin):
        xn1F = din(nc, "xn1F", [DM, TT])
        xn1T = din(nc, "xn1T", [TT, DM], BF16)
        wsw = din(nc, "wsw", [DM, 8])
        bswR = din(nc, "bswR", [128, 8])
        w1e = din(nc, "w1e", [DM, 4096])
        b1R = din(nc, "b1R", [128, 32])
        w2e = din(nc, "w2e", [4096, DM])
        b2R = din(nc, "b2R", [128, 8])
        myhot = din(nc, "myhot", [8, 1])
        identf = din(nc, "identf", [128, 128])
        identb = din(nc, "identb", [128, 128], BF16)
        iota8D = din(nc, "iota8", [128, NTT * 8])
        ctok = din(nc, "ctok", [128, 2 * TT // 128])
        gidxD = nc.dram_tensor("gidx1", [CAP1, 2], F32, kind="Internal").ap()
        rowD = nc.dram_tensor("rowD1", [1, 8192], F32, kind="Internal").ap()
        o_ye = dout(nc, "ye", [CAP1, DM], BF16)
        o_meta = dout(nc, "meta", [128, NTT * 2])

        s = begin()
        setup_consts(nc, s, sb)
        psA = ps("psA", [128, 2048])
        psT = ps("psT", [128, 1024], BF16)
        bcreg = {}
        def mk_regs(g):
            r1 = g.alloc_register("bc_slot"); g.reg_mov(r1, CAP1 - 1)
            r2 = g.alloc_register("bc_tok"); g.reg_mov(r2, TT - 1)
            bcreg["slot"], bcreg["tok"] = r1, r2
            return None
        s.issue("gpsimd", mk_regs, incs=0)

        SC = [sb(f"SC{i}", [128, 8192]) for i in range(2)]
        XB = [sb(f"XB{i}", [128, 2048]) for i in range(8)]
        lg = sb("lg", [128, NTT * 8])
        pr = sb("pr", [128, NTT * 8])
        pfA = sb("pfA", [128, NTT * 8]); pfB = sb("pfB", [128, NTT * 8])
        t1s = sb("t1s", [128, NTT * 8])
        iota8 = sb("iota8s", [128, NTT * 8])
        sif = sb("sif", [128, 128]); sib = sb("sib", [128, 128], BF16)
        swk = [sb(f"swk{k}", [128, 8]) for k in range(8)]
        sbsw = sb("sbsw", [128, 8])
        smy = sb("smy", [8, 1])
        sctok = sb("sctok", [128, 2 * TT // 128])
        sb1 = sb("sb1", [128, 32]); sb2 = sb("sb2", [128, 8])
        s.dma("sync", iota8[:], iota8D)
        s.dma("sync", sif[:], identf)
        s.dma("sync", sib[:], identb)
        s.dma("sync", smy[:], myhot)
        s.dma("sync", sctok[:], ctok)
        s.dma("sync", sbsw[:], bswR)
        s.dma("sync", sb1[:], b1R)
        s.dma("sync", sb2[:], b2R)
        for k in range(8):
            s.dma("sync", swk[k][:], wsw[k*128:(k+1)*128, :])
        s.barrier()

        for ch in range(8):
            tc0 = ch * 2048
            for k in range(8):
                s.dma("sync", XB[k][:], xn1F[k*128:(k+1)*128, tc0:tc0+2048])
            s.barrier()
            for tt in range(16):
                def mmlg(tensor, ch=ch, tt=tt):
                    last = None
                    for k in range(8):
                        last = tensor.matmul(psA[:, (tt % 4)*512:(tt % 4)*512+8],
                                             XB[k][:, tt*128:(tt+1)*128], swk[k][:],
                                             start=(k == 0), stop=(k == 7))
                    return last
                s.issue("tensor", mmlg)
                if tt % 4 == 3:
                    s.barrier()
                    for t4 in range(tt-3, tt+1):
                        gt = ch * 16 + t4
                        s.issue("vector", lambda v, t4=t4, gt=gt: v.tensor_tensor(
                            lg[:, gt*8:(gt+1)*8], psA[:, (t4 % 4)*512:(t4 % 4)*512+8],
                            sbsw[:, 0:8], ALU.add))
                    s.barrier()

        v3 = lambda x: x[:].rearrange("p (t e) -> p t e", e=8)
        bc = lambda x: x[:, :, None].broadcast_to([128, NTT, 8])
        mx = sb("mx", [128, NTT])
        estar = sb("estar", [128, NTT])
        oh1 = sb("oh1", [128, NTT * 8])
        s.issue("vector", lambda v: v.tensor_reduce(mx[:], v3(lg), AX.X, ALU.max))
        s.barrier()
        s.issue("vector", lambda v: v.tensor_tensor(v3(oh1), v3(lg), bc(mx), ALU.is_ge))
        s.barrier()
        s.issue("vector", lambda v: v.tensor_copy(pfA[:], oh1[:]))
        s.barrier()
        for sh, (a, b) in zip((1, 2, 4), ((pfA, pfB), (pfB, pfA), (pfA, pfB))):
            s.issue("vector", lambda v, sh=sh, a=a, b=b: v.tensor_copy(
                v3(b)[:, :, 0:sh], v3(a)[:, :, 0:sh]))
            s.barrier()
            s.issue("vector", lambda v, sh=sh, a=a, b=b: v.tensor_tensor(
                v3(b)[:, :, sh:8], v3(a)[:, :, sh:8], v3(a)[:, :, 0:8-sh], ALU.add))
            s.barrier()
        s.issue("vector", lambda v: v.tensor_scalar(out=pfA[:], in0=pfB[:], scalar1=1.0,
                                                    scalar2=None, op0=ALU.is_le))
        s.barrier()
        s.issue("vector", lambda v: v.tensor_tensor(oh1[:], oh1[:], pfA[:], ALU.mult))
        s.barrier()
        s.issue("vector", lambda v: v.tensor_tensor(t1s[:], oh1[:], iota8[:], ALU.mult))
        s.barrier()
        s.issue("vector", lambda v: v.tensor_reduce(estar[:], v3(t1s), AX.X, ALU.add))
        s.barrier()

        oh_em = SC[0][0:8, :]
        rk_em = SC[1][0:8, :]
        carry = sb("carry", [8, 1])
        R8 = sb("R8", [1, 4096])
        mr128 = sb("mr128", [128, 64]); mo128 = sb("mo128", [128, 64])
        slotf = sb("slotf", [128, 64]); sloti = sb("sloti", [128, 64], I32)
        kp = sb("kp", [128, 64])
        rktok = sb("rktok", [128, NTT])
        s.issue("vector", lambda v: v.memset(mr128[:], 1e9))
        s.issue("vector", lambda v: v.memset(carry[:], 0.0))
        s.barrier()
        s.dma("sync", gidxD[0:CAP1, :].rearrange("(p c) x -> p (c x)", p=128), mr128[:, 0:16])
        s.barrier()
        for ch in range(ECH):
            tt0 = ch * 64
            for half in range(4):
                for i in range(16):
                    tt = tt0 + half * 16 + i
                    slot = i % 8
                    s.issue("tensor", lambda te, tt=tt, slot=slot: te.transpose(
                        psA[0:8, slot*128:(slot+1)*128], oh1[:, tt*8:(tt+1)*8], sif[:]))
                    if i % 8 == 7:
                        s.barrier()
                        for ii in range(i - 7, i + 1):
                            slot2 = ii % 8
                            col0 = (half * 16 + ii) * 128
                            s.issue("vector", lambda v, slot2=slot2, col0=col0: v.tensor_copy(
                                oh_em[:, col0:col0+128],
                                psA[0:8, slot2*128:(slot2+1)*128]))
                        s.barrier()
            s.issue("vector", lambda v: v.tensor_tensor_scan(
                rk_em[:], oh_em[:], oh_em[:], carry[0:8, 0:1], ALU.add, ALU.bypass))
            s.barrier()
            s.issue("vector", lambda v: v.tensor_copy(carry[:], rk_em[:, 8191:8192]))
            s.barrier()
            s.issue("vector", lambda v: v.tensor_tensor(rk_em[:], rk_em[:], oh_em[:], ALU.subtract))
            s.barrier()
            for which, dst in ((0, mr128), (1, mo128)):
                srcT = rk_em if which == 0 else oh_em
                for hb in range(2):
                    for i in range(8):
                        ii = hb * 8 + i
                        s.issue("tensor", lambda te, i=i, ii=ii, srcT=srcT: te.matmul(
                            psA[0:1, (i % 4)*512:(i % 4)*512+512], smy[:, 0:1],
                            srcT[:, ii*512:(ii+1)*512], start=True, stop=True))
                        if i % 4 == 3:
                            s.barrier()
                            for i2 in range(i-3, i+1):
                                s.issue("vector", lambda v, i2=i2: v.tensor_copy(
                                    R8[:, i2*512:(i2+1)*512], psA[0:1, (i2 % 4)*512:(i2 % 4)*512+512]))
                            s.barrier()
                    s.dma("sync", rowD[:, 0:4096], R8[:])
                    s.barrier()
                    s.dma("sync", dst[:, hb*32:(hb+1)*32],
                          rowD[:, 0:4096].rearrange("x (p c) -> (x p) c", p=128))
                    s.barrier()
            s.issue("vector", lambda v: v.tensor_scalar(out=kp[:], in0=mr128[:], scalar1=float(CAP1),
                                                        scalar2=None, op0=ALU.is_lt))
            s.barrier()
            s.issue("vector", lambda v: v.tensor_tensor(kp[:], kp[:], mo128[:], ALU.mult))
            s.barrier()
            s.issue("vector", lambda v: v.tensor_scalar(out=slotf[:], in0=kp[:], scalar1=-1e9,
                                                        scalar2=1e9, op0=ALU.mult, op1=ALU.add))
            s.barrier()
            s.issue("vector", lambda v: v.tensor_tensor(kp[:], mr128[:], kp[:], ALU.mult))
            s.barrier()
            s.issue("vector", lambda v: v.tensor_tensor(slotf[:], slotf[:], kp[:], ALU.add))
            s.barrier()
            s.issue("vector", lambda v: v.tensor_copy(sloti[:], slotf[:]))
            s.barrier()
            for j in range(64):
                s.issue("gpsimd", lambda g, ch=ch, j=j: g.indirect_dma_start(
                    out=gidxD, out_offset=IndirectOffsetOnAxis(sloti[:, j:j+1], 0),
                    in_=sctok[:, (ch*64+j)*2:(ch*64+j)*2+2], in_offset=None,
                    bounds_check=bcreg["slot"], oob_is_err=False), incs=16)
                if j % 4 == 3:
                    s.barrier()
            s.barrier()
            s.issue("vector", lambda v: v.tensor_tensor(rk_em[:], rk_em[:], oh_em[:], ALU.mult))
            s.barrier()
            for hb in range(2):
                for i in range(8):
                    ii = hb * 8 + i
                    s.issue("tensor", lambda te, i=i, ii=ii: te.matmul(
                        psA[0:1, (i % 4)*512:(i % 4)*512+512], s.ones_col[0:8, 0:1],
                        rk_em[:, ii*512:(ii+1)*512], start=True, stop=True))
                    if i % 4 == 3:
                        s.barrier()
                        for i2 in range(i-3, i+1):
                            s.issue("vector", lambda v, i2=i2: v.tensor_copy(
                                R8[:, i2*512:(i2+1)*512], psA[0:1, (i2 % 4)*512:(i2 % 4)*512+512]))
                        s.barrier()
                s.dma("sync", rowD[:, 0:4096], R8[:])
                s.barrier()
                s.dma("sync", mr128[0:32, 0:128].rearrange("p c -> p c") if False else
                      SC[0][0:32, 0:128], rowD[:, 0:4096].rearrange("x (t p) -> (x t) p", p=128))
                s.barrier()
                s.issue("tensor", lambda te: te.transpose(psA[0:128, 0:32],
                                                          SC[0][0:32, 0:128], sif[0:32, 0:32]))
                s.barrier()
                s.issue("vector", lambda v, ch=ch, hb=hb: v.tensor_copy(
                    rktok[:, ch*64 + hb*32:ch*64 + (hb+1)*32], psA[0:128, 0:32]))
                s.barrier()
        s.dma("sync", o_meta[:, 0:NTT], estar[:])
        s.dma("sync", o_meta[:, NTT:2*NTT], rktok[:])
        s.barrier()

        for k in range(8):
            bv = XB[k][:].bitcast(BF16)
            s.dma("gpsimd", bv[:, 0:4096], w1e[k*128:(k+1)*128, :])
        gidxs = sb("gidxs", [128, 16]); gidxi = sb("gidxi", [128, 16], I32)
        s.dma("sync", gidxs[:].rearrange("p (j x) -> p j x", x=2),
              gidxD[0:CAP1, :].rearrange("(j p) x -> p j x", p=128))
        s.barrier()
        s.issue("vector", lambda v: v.tensor_copy(gidxi[:], gidxs[:]))
        s.barrier()
        w1v = [XB[k][:].bitcast(BF16)[:, 0:4096] for k in range(8)]
        xg = pr[:].bitcast(BF16)[:, 0:1024]
        xf = [sb(f"xf{k}", [128, 1024], BF16) for k in range(8)]
        yfm = [sb(f"yfm{k}", [128, 1024], BF16) for k in range(6)]
        _lgv = lg[:].bitcast(BF16)
        yfm.append(_lgv[:, 0:1024]); yfm.append(_lgv[:, 1024:2048])
        _pfBv = pfB[:].bitcast(BF16)
        w2b = [_pfBv[:, k*128:(k+1)*128] for k in range(16)]
        h1v = []
        for i in range(2):
            bv = SC[i][:].bitcast(BF16)
            for j in range(16):
                h1v.append(bv[:, j*1024:(j+1)*1024])
        for g in range(8):
            s.issue("vector", lambda v: v.memset(xg, 0.0))
            s.barrier()
            s.issue("gpsimd", lambda gp, g=g: gp.indirect_dma_start(
                out=xg, out_offset=None, in_=xn1T,
                in_offset=IndirectOffsetOnAxis(gidxi[:, g*2:g*2+1], 0),
                bounds_check=bcreg["tok"], oob_is_err=False), incs=16)
            s.barrier()
            for k in range(8):
                s.issue("tensor", lambda te, k=k: te.transpose(
                    psT[:, (k % 8)*128:(k % 8 + 1)*128], xg[:, k*128:(k+1)*128], sib[:]))
            s.barrier()
            for k in range(8):
                s.issue("vector", lambda v, g=g, k=k: v.tensor_copy(
                    xf[k][:, g*128:(g+1)*128], psT[:, (k % 8)*128:(k % 8 + 1)*128]))
            s.barrier()
        for m in range(32):
            for f in range(2):
                def mmh(tensor, m=m, f=f):
                    last = None
                    for k in range(8):
                        last = tensor.matmul(psA[:, (f % 4)*512:(f % 4 + 1)*512],
                                             w1v[k][:, m*128:(m+1)*128],
                                             xf[k][:, f*512:(f+1)*512],
                                             start=(k == 0), stop=(k == 7))
                    return last
                s.issue("tensor", mmh)
            s.barrier()
            for f in range(2):
                s.issue("scalar", lambda scl, m=m, f=f: scl.activation(
                    out=h1v[m][:, f*512:(f+1)*512], in_=psA[:, (f % 4)*512:(f % 4 + 1)*512],
                    func=AF.Relu, bias=sb1[:, m:m+1]))
            s.barrier()
        for m in range(8):
            for rnd in range(2):
                for kb in range(16):
                    s.dma("gpsimd", w2b[kb], w2e[(rnd*16+kb)*128:(rnd*16+kb+1)*128,
                                                 m*128:(m+1)*128])
                s.barrier()
                for f in range(2):
                    def mmy(tensor, m=m, f=f, rnd=rnd):
                        last = None
                        for kb in range(16):
                            last = tensor.matmul(psA[:, (f % 4)*512:(f % 4 + 1)*512],
                                                 w2b[kb], h1v[rnd*16+kb][:, f*512:(f+1)*512],
                                                 start=(rnd == 0 and kb == 0),
                                                 stop=(rnd == 1 and kb == 15))
                        return last
                    s.issue("tensor", mmy)
                s.barrier()
            for f in range(2):
                s.issue("scalar", lambda scl, m=m, f=f: scl.activation(
                    out=yfm[m][:, f*512:(f+1)*512], in_=psA[:, (f % 4)*512:(f % 4 + 1)*512],
                    func=AF.Identity, bias=sb2[:, m:m+1]))
            s.barrier()
        for g in range(8):
            for k in range(8):
                s.issue("tensor", lambda te, g=g, k=k: te.transpose(
                    psT[:, (k % 8)*128:(k % 8 + 1)*128], yfm[k][:, g*128:(g+1)*128], sib[:]))
            s.barrier()
            for k in range(8):
                s.issue("vector", lambda v, g=g, k=k: v.tensor_copy(
                    xg[:, k*128:(k+1)*128], psT[:, (k % 8)*128:(k % 8 + 1)*128]))
            s.barrier()
            s.dma("sync", o_ye[g*128:(g+1)*128, :], xg)
            s.barrier()
    return nc


def _prep_moe1a(inputs, xn1_cores_f32, xn1b_cores):
    xn1F = _cores_to_ett(xn1_cores_f32, np.float32)
    xn1T = np.ascontiguousarray(_cores_to_ett(xn1b_cores, BF).T)
    NTT = TT // 128
    iota8 = np.tile(np.arange(8, dtype=np.float32), (128, NTT))
    ctok = (np.arange(TT, dtype=np.float32)
            .reshape(2, 2, 128, 32).transpose(2, 0, 1, 3).reshape(128, 128))
    ctok = np.repeat(ctok, 2, axis=1)
    identf = np.eye(128, dtype=np.float32)
    identb = np.eye(128, dtype=ml_dtypes.bfloat16)
    bswR = np.broadcast_to(inputs["L1_ffn_b_switch"], (128, 8)).copy()
    maps = []
    for c in range(NC):
        myhot = np.zeros((8, 1), np.float32); myhot[c] = 1.0
        b1R = np.ascontiguousarray(inputs["L1_ffn_b1"][c].reshape(32, 128).T)
        b2R = np.ascontiguousarray(inputs["L1_ffn_b2"][c].reshape(8, 128).T)
        maps.append({"xn1F": xn1F, "xn1T": xn1T, "wsw": inputs["L1_ffn_w_switch"],
                     "bswR": bswR, "w1e": np.ascontiguousarray(inputs["L1_ffn_w1"][c]),
                     "b1R": b1R, "w2e": np.ascontiguousarray(inputs["L1_ffn_w2"][c]),
                     "b2R": b2R, "myhot": myhot, "identf": identf, "identb": identb,
                     "iota8": iota8, "ctok": ctok, **CONSTS})
    return maps


def build_moe1b():
    with program("moe1b") as (nc, sb, ps, begin):
        yAll = din(nc, "yAll", [NE * CAP1, DM], BF16)
        meta2 = din(nc, "meta2", [128, 32])
        xn1TF = din(nc, "xn1TF", [TK, DM])
        s1F = din(nc, "s1F", [DM, TK], BF16)
        x1D = din(nc, "x1D", [DM, TK])
        identb = din(nc, "identb", [128, 128], BF16)
        o_x2 = dout(nc, "x2", [DM, TK])

        s = begin()
        setup_consts(nc, s, sb)
        psT = ps("psT", [128, 1024], BF16)
        bcreg = {}
        def mk_regs(g):
            r = g.alloc_register("bc_y"); g.reg_mov(r, NE * CAP1 - 1)
            bcreg["y"] = r
            return None
        s.issue("gpsimd", mk_regs, incs=0)
        sm = sb("sm", [128, 32])
        sib = sb("sib", [128, 128], BF16)
        s.dma("sync", sm[:], meta2)
        s.dma("sync", sib[:], identb)
        s.barrier()
        estar = sm[:, 0:16]; rk = sm[:, 16:32]
        ff = sb("ff", [128, 16]); kpt = sb("kpt", [128, 16]); km1 = sb("km1", [128, 16])
        idx = sb("idx", [128, 16], I32)
        s.issue("vector", lambda v: v.scalar_tensor_tensor(
            out=ff[:], in0=estar, scalar=float(CAP1), in1=rk, op0=ALU.mult, op1=ALU.add))
        s.barrier()
        s.issue("vector", lambda v: v.tensor_scalar(
            out=kpt[:], in0=rk, scalar1=float(CAP1), scalar2=None, op0=ALU.is_lt))
        s.barrier()
        s.issue("vector", lambda v: v.tensor_tensor(ff[:], ff[:], kpt[:], ALU.mult))
        s.barrier()
        s.issue("vector", lambda v: v.tensor_scalar(
            out=km1[:], in0=kpt[:], scalar1=-1.0, scalar2=1.0, op0=ALU.mult, op1=ALU.add))
        s.barrier()
        s.issue("vector", lambda v: v.scalar_tensor_tensor(
            out=ff[:], in0=km1[:], scalar=1e9, in1=ff[:], op0=ALU.mult, op1=ALU.add))
        s.barrier()
        s.issue("vector", lambda v: v.tensor_copy(idx[:], ff[:]))
        s.barrier()
        ybf = sb("ybf", [128, DM], BF16)
        xpre = sb("xpre", [128, DM])
        gtm = sb("gtm", [128, DM])
        gb = sb("gb", [128, DM], BF16)
        gfm = [sb(f"gfm{k}", [128, TK], BF16) for k in range(8)]
        for tt in range(16):
            s.issue("vector", lambda v: v.memset(ybf[:], 0.0))
            s.barrier()
            s.dma("sync", xpre[:], xn1TF[tt*128:(tt+1)*128, :])
            s.issue("gpsimd", lambda g, tt=tt: g.indirect_dma_start(
                out=ybf[:], out_offset=None, in_=yAll,
                in_offset=IndirectOffsetOnAxis(idx[:, tt:tt+1], 0),
                bounds_check=bcreg["y"], oob_is_err=False), incs=16)
            s.barrier()
            s.issue("vector", lambda v, tt=tt: v.tensor_scalar(
                out=xpre[:], in0=xpre[:], scalar1=km1[:, tt:tt+1], scalar2=None, op0=ALU.mult))
            s.barrier()
            s.issue("vector", lambda v: v.tensor_tensor(gtm[:], xpre[:], ybf[:], ALU.add))
            s.barrier()
            s.issue("scalar", lambda sc2: sc2.activation(out=gtm[:], in_=gtm[:], func=AF.Silu))
            s.barrier()
            s.issue("vector", lambda v: v.tensor_copy(gb[:], gtm[:]))
            s.barrier()
            for k in range(8):
                s.issue("tensor", lambda te, k=k: te.transpose(
                    psT[:, (k % 8)*128:(k % 8 + 1)*128], gb[:, k*128:(k+1)*128], sib[:]))
            s.barrier()
            for k in range(8):
                s.issue("vector", lambda v, k=k, tt=tt: v.tensor_copy(
                    gfm[k][:, tt*128:(tt+1)*128], psT[:, (k % 8)*128:(k % 8 + 1)*128]))
            s.barrier()
        xk = sb("xk", [128, TK])
        sk = sb("sk", [128, TK], BF16)
        for k in range(8):
            s.dma("sync", xk[:], x1D[k*128:(k+1)*128, :])
            s.dma("sync", sk[:], s1F[k*128:(k+1)*128, :])
            s.barrier()
            s.issue("vector", lambda v, k=k: v.tensor_mul(gfm[k][:], sk[:], gfm[k][:]))
            s.barrier()
            s.issue("vector", lambda v, k=k: v.tensor_tensor(xk[:], xk[:], gfm[k][:], ALU.add))
            s.barrier()
            s.dma("sync", o_x2[k*128:(k+1)*128, :], xk[:])
            s.barrier()
    return nc


def _prep_moe1b(inputs, meta1_full, xn1_cores_f32, s1_cores, x1_cores):
    identb = np.eye(128, dtype=ml_dtypes.bfloat16)
    NTT = TT // 128
    maps = []
    for c in range(NC):
        b, q = c // 4, c % 4
        tt0 = (b * L + q * TK) // 128
        meta2 = np.ascontiguousarray(np.concatenate(
            [meta1_full[:, tt0:tt0+16], meta1_full[:, NTT+tt0:NTT+tt0+16]], axis=1))
        xn1TF = np.ascontiguousarray(np.asarray(xn1_cores_f32[c]).T)
        maps.append({"yAll": None, "meta2": meta2, "xn1TF": xn1TF, "s1F": s1_cores[c],
                     "x1D": x1_cores[c], "identb": identb, **CONSTS})
    return maps


def _prep_entry(inputs):
    x = inputs["L0_attn_x"]
    WiT = np.ascontiguousarray(inputs["L0_attn_Wi"].T)
    biR = np.ascontiguousarray(inputs["L0_attn_bi"].reshape(8, 128).T)
    maps = []
    for c in range(NC):
        b, q = c // 4, c % 4
        xT = np.ascontiguousarray(x[b, q*TK:(q+1)*TK, :].T)
        maps.append({"xT": xT, "WiT": WiT, "biR": biR, **CONSTS})
    return maps


def _fm_to_full(cores, dtype=np.float32):
    out = np.empty((B, L, DM), dtype)
    for c in range(NC):
        b, q = c // 4, c % 4
        out[b, q*TK:(q+1)*TK] = np.asarray(cores[c], dtype).T
    return out


def _cores_to_ett(cores, dtype=np.float32):
    rows = cores[0].shape[0]
    out = np.empty((rows, TT), dtype)
    for c in range(NC):
        b, q = c // 4, c % 4
        out[:, b*L + q*TK : b*L + (q+1)*TK] = np.asarray(cores[c], dtype)
    return out


_PROGS = {}

def _prog(name):
    if name not in _PROGS:
        _PROGS[name] = globals()["build_" + name]()
    return _PROGS[name]


def _run(name, maps):
    return run_bass_kernel_spmd(_prog(name), maps, core_ids=list(range(NC)))


def l0_sbranch(inputs):
    r = _run("entry", _prep_entry(inputs))
    xn_cores = [np.asarray(rr["xn_fm"]) for rr in r.results]
    h_full = _fm_to_full([rr["h_fm"] for rr in r.results])
    for blk in range(NB):
        ra = _run("alpha", _prep_alpha(inputs, blk, h_full))
        delta_full = _cores_to_ett([rr["delta"] for rr in ra.results])
        u_cores = [np.asarray(rr["u_s"]) for rr in ra.results]
        z_cores = [np.asarray(rr["z_s"]) for rr in ra.results]
        u_full = _cores_to_ett(u_cores, BF)
        bc = _cores_to_ett([rr["BC"] for rr in ra.results])
        rb = _run("beta", _prep_beta(inputs, blk, delta_full, u_full,
                                     np.ascontiguousarray(bc[:32]), np.ascontiguousarray(bc[32:])))
        y_full = np.concatenate([np.asarray(rr["y"]) for rr in rb.results], axis=0)
        rg = _run("gamma", _prep_gamma(inputs, blk, y_full, u_cores, z_cores, h_full))
        h_full = _fm_to_full([rr["h2"] for rr in rg.results])
    fnG = np.ascontiguousarray(inputs["L0_attn_fn_g"].reshape(8, 128).T)
    fnB = np.ascontiguousarray(inputs["L0_attn_fn_b"].reshape(8, 128).T)
    fmaps = []
    for c in range(NC):
        b, q = c // 4, c % 4
        fmaps.append({"hD": np.ascontiguousarray(h_full[b, q*TK:(q+1)*TK, :].T),
                      "fnG": fnG, "fnB": fnB, **CONSTS})
    rf = _run("fin", fmaps)
    s_cores = [np.asarray(rr["s_fm"]) for rr in rf.results]
    return s_cores, xn_cores


def kernel(**inputs):
    inputs = {k: np.asarray(v) for k, v in inputs.items()}
    s_cores, xn_cores = l0_sbranch(inputs)
    ra = _run("moe0a", _prep_moe0a(inputs, xn_cores))
    meta = np.asarray(ra.results[0]["meta"])
    yAll = np.concatenate([np.asarray(ra.results[c]["ye"]) for c in range(NC)], axis=0)
    mb = _prep_moe0b(inputs, meta, s_cores, inputs["L0_attn_x"])
    for m in mb:
        m["yAll"] = yAll
    rb = _run("moe0b", mb)
    x1_cores = [np.asarray(r["x1"]) for r in rb.results]
    r1 = _run("att1", _prep_att1(inputs, x1_cores))
    xn1_cores = [np.asarray(r["xn1"]) for r in r1.results]
    xn1b_cores = [np.asarray(r["xn1b"]) for r in r1.results]
    q_cores = [np.asarray(r["q"]) for r in r1.results]
    k_cores = [np.asarray(r["k"]) for r in r1.results]
    v_cores = [np.asarray(r["v"]) for r in r1.results]
    r2 = _run("att2", _prep_att2(inputs, q_cores, k_cores, v_cores))
    s1_cores = [np.asarray(r["s1"]) for r in r2.results]
    rm1 = _run("moe1a", _prep_moe1a(inputs, xn1_cores, xn1b_cores))
    meta1 = np.asarray(rm1.results[0]["meta"])
    yAll1 = np.concatenate([np.asarray(rm1.results[c]["ye"]) for c in range(NC)], axis=0)
    mb1 = _prep_moe1b(inputs, meta1, xn1_cores, s1_cores, x1_cores)
    for m in mb1:
        m["yAll"] = yAll1
    rb1 = _run("moe1b", mb1)
    out = _fm_to_full([r["x2"] for r in rb1.results])
    return out
